# version 28
# baseline (speedup 1.0000x reference)
import contextlib

import numpy as np
import concourse.bass as bass
import concourse.mybir as mybir
from concourse.bass_utils import run_bass_kernel_spmd
from concourse.alu_op_type import AluOpType as OP

F32 = mybir.dt.float32
BF16 = mybir.dt.bfloat16
AF = mybir.ActivationFunctionType
AX = mybir.AxisListType

B, T, D = 2, 4096, 4096
NH, NKV, HD = 32, 8, 128
SCALING = HD ** -0.5
CSH = -14.0
EPS = 1e-5
TQ = 1024
NTT = TQ // 128
NSR = T // 512
NQH = 8
NCK = 8

ENGINES = ("tensor", "vector", "scalar", "gpsimd", "sync")


class Prog:

    def __init__(self, nc):
        self.nc = nc
        self.ops = []
        self.eng_count = {e: 0 for e in ENGINES}
        self.dma_count = {e: 0 for e in ENGINES}
        self.cc_count = 0
        self.hwm = {e: {} for e in ENGINES}
        self.bufs = {}
        self._fence = {}

    def buf(self, name):
        if name not in self.bufs:
            self.bufs[name] = [None, []]
        return self.bufs[name]

    def fence(self):
        for e in ENGINES:
            if self.eng_count[e]:
                self._fence["e_" + e] = self.eng_count[e]
            if self.dma_count[e]:
                self._fence["q_" + e] = self.dma_count[e] * 16
        if self.cc_count:
            self._fence["cc"] = self.cc_count
        self._fence = dict(self._fence)

    def op(self, engine, fn, reads=(), writes=(), is_dma=False, is_cc=False,
           extra_waits=()):
        waits = {}

        def add(tok, waw_self=False):
            if tok is None:
                return
            s, c = tok
            if s.startswith("q_"):
                qeng = s[2:]
                if waw_self and is_dma and qeng == engine:
                    return
                c = 16 * self.dma_count[qeng]
                if c == 0:
                    return
            if waits.get(s, 0) < c:
                waits[s] = c

        for b in reads:
            add(self.buf(b)[0])
        for b in writes:
            bb = self.buf(b)
            add(bb[0], waw_self=True)
            for t in bb[1]:
                add(t)
        for t in extra_waits:
            add(t)
        for s, c in self._fence.items():
            add((s, c))
        hw = self.hwm[engine]
        final = []
        for s, c in waits.items():
            if engine == "tensor" and s == "e_tensor":
                continue
            if hw.get(s, 0) >= c:
                continue
            final.append((s, c))
            hw[s] = c
        if is_cc:
            tok = ("cc", self.cc_count + 1)
            self.cc_count += 1
        elif is_dma:
            tok = ("q_" + engine, (self.dma_count[engine] + 1) * 16)
            self.dma_count[engine] += 1
        else:
            tok = ("e_" + engine, self.eng_count[engine] + 1)
            self.eng_count[engine] += 1
        self.ops.append((engine, fn, final, tok, is_dma or is_cc))
        for b in writes:
            bb = self.buf(b)
            bb[0] = tok
            bb[1] = []
        for b in reads:
            self.buf(b)[1].append(tok)
        return tok

    def build(self, end_engine="sync"):
        nc = self.nc
        stack = contextlib.ExitStack()
        sems = {}
        for e in ENGINES:
            if self.eng_count[e]:
                sems["e_" + e] = stack.enter_context(nc.semaphore("e_" + e))
            if self.dma_count[e] or e == end_engine:
                sems["q_" + e] = stack.enter_context(nc.semaphore("q_" + e))
        if self.cc_count:
            sems["cc"] = stack.enter_context(nc.semaphore("cc"))
        finals = {}
        for e in ENGINES:
            if self.eng_count[e]:
                finals["e_" + e] = self.eng_count[e]
            if self.dma_count[e]:
                finals["q_" + e] = self.dma_count[e] * 16
        if self.cc_count:
            finals["cc"] = self.cc_count
        per_eng = {e: [] for e in ENGINES}
        for rec in self.ops:
            per_eng[rec[0]].append(rec)
        block = stack.enter_context(nc.Block())

        def make_body(eng_name, recs):
            def body(eng):
                for _, fn, fwaits, tok, is_dma in recs:
                    for s, c in fwaits:
                        eng.wait_ge(sems[s], c)
                    instr = fn(eng)
                    instr.then_inc(sems[tok[0]],
                                   16 if (is_dma and tok[0].startswith("q_")) else 1)
                if eng_name == end_engine:
                    for s, c in finals.items():
                        if s == "e_" + end_engine:
                            continue
                    for s, c in finals.items():
                        if s == "e_" + end_engine:
                            continue
                        eng.wait_ge(sems[s], c)
            return body

        for e in ENGINES:
            if per_eng[e] or e == end_engine:
                getattr(block, e)(make_body(e, per_eng[e]))
        return stack


def build_nc():
    nc = bass.Bass("TRN2", target_bir_lowering=False, debug=False)

    xq = nc.dram_tensor("xq", [TQ, D], F32, kind="ExternalInput").ap()
    wq = nc.dram_tensor("wq", [128, 8 * 32 * 128], F32, kind="ExternalInput").ap()
    wk = nc.dram_tensor("wk", [128, 2 * 32 * 128], F32, kind="ExternalInput").ap()
    wv = nc.dram_tensor("wv", [128, 32 * 256], F32, kind="ExternalInput").ap()
    wo = nc.dram_tensor("wo", [128, 8, 8 * 512], F32, kind="ExternalInput").ap()
    awr = nc.dram_tensor("awr", [128, 32 * 132], F32, kind="ExternalInput").ap()
    bst = nc.dram_tensor("bst", [128, 4096], F32, kind="ExternalInput").ap()
    ropec = nc.dram_tensor("ropec", [128, T], F32, kind="ExternalInput").ap()
    ropes = nc.dram_tensor("ropes", [128, T], F32, kind="ExternalInput").ap()
    maskin = nc.dram_tensor("maskin", [128, 2048], F32, kind="ExternalInput").ap()
    identin = nc.dram_tensor("identin", [128, 128], F32, kind="ExternalInput").ap()
    out = nc.dram_tensor("out", [TQ, D], F32, kind="ExternalOutput").ap()

    lnfullA = nc.dram_tensor("lnfullA", [4 * D, 512], BF16)
    lnfullB = nc.dram_tensor("lnfullB", [4 * D, 512], BF16)
    lncA = nc.dram_tensor("lncA", [D, 512], BF16)
    lncB = nc.dram_tensor("lncB", [D, 512], BF16)
    qspill = nc.dram_tensor("qspill", [NQH * 128, T], BF16)
    pck = [nc.dram_tensor(f"pck{k}", [8, 128, 2048], F32) for k in range(NCK)]
    rck = [nc.dram_tensor(f"rck{k}", [2, 128, 2048], F32) for k in range(NCK)]

    p = Prog(nc)
    st = contextlib.ExitStack()

    P0 = 16512
    def at(name, shape, dtype, off):
        return nc.alloc_sbuf_tensor_at(name, shape, dtype, offset=off)

    K = at("K", [128, 2 * T], BF16, P0)
    V = at("V", [128, 32 * 256], BF16, P0 + 16384)
    SM = P0 + 32768
    ones = at("ones", [128, 128], BF16, SM)
    biasc = at("biasc", [128, 1], F32, SM + 256)
    epst = at("epst", [128, 1], F32, SM + 288)
    stats = [at(f"stats{i}", [128, 48], F32, SM + 320 + 192 * i) for i in range(2)]
    mv = [at(f"mv{i}", [128, 2], F32, SM + 704 + 32 * i) for i in range(2)]
    rstd = [at(f"rstd{i}", [128, 1], F32, SM + 768 + 32 * i) for i in range(2)]
    negm = [at(f"negm{i}", [128, 1], F32, SM + 832 + 32 * i) for i in range(2)]
    identb = at("identb", [128, 128], BF16, SM + 896)
    D0 = SM + 1280
    DEND = 229300

    pA = st.enter_context(nc.psum_tensor("pA", [128, 1024], F32))
    pB = st.enter_context(nc.psum_tensor("pB", [128, 1024], F32))
    pC = [st.enter_context(nc.psum_tensor(f"pC{i}", [128, 512], F32)) for i in range(4)]
    pAb = pA[:].bitcast(BF16)
    pBb = pB[:].bitcast(BF16)

    p.op("vector", lambda v: v.memset(ones[:], 1.0), writes=["ones"])
    p.op("vector", lambda v: v.memset(biasc[:], CSH), writes=["biasc"])
    p.op("vector", lambda v: v.memset(epst[:], EPS), writes=["epst"])
    p.op("gpsimd", lambda g: g.dma_start(out=identb[:], in_=identin[:]),
         writes=["identb"], is_dma=True)

    cur = [D0]
    def alloc(name, shape, dtype, nbytes):
        nbytes = (nbytes + 31) // 32 * 32
        t = at(name, shape, dtype, cur[0])
        cur[0] += nbytes
        assert cur[0] <= DEND, f"SBUF overflow at {name}: {cur[0]}"
        return t

    xt = [alloc(f"xt{i}", [128, D], F32, 16384) for i in range(2)]
    ln1tm = alloc("ln1tm", [128, NTT * D], BF16, NTT * D * 2 // 1)
    lnstage = [alloc(f"lnstage{i}", [128, 2048], BF16, 4096) for i in range(2)]

    def emit_ln(xtile, xname, outtile, outname, outoff, par=0):
        stt, mvt, rst, ngt = stats[par], mv[par], rstd[par], negm[par]
        sn, mn, rn, nn = f"stats{par}", f"mv{par}", f"rstd{par}", f"negm{par}"
        for i in range(8):
            p.op("vector", lambda v, i=i, xtile=xtile, stt=stt: v.bn_stats(
                out=stt[:, i * 6:(i + 1) * 6], in_=xtile[:, i * 512:(i + 1) * 512]),
                reads=[xname], writes=[sn])
        p.op("vector", lambda v, stt=stt, mvt=mvt: v.bn_aggr(out=mvt[:], in_=stt[:]),
             reads=[sn], writes=[mn])
        p.op("scalar", lambda s, mvt=mvt, rst=rst: s.activation(out=rst[:], in_=mvt[:, 1:2],
             func=AF.Sqrt, bias=epst[:]), reads=[mn, "epst"], writes=[rn])
        p.op("vector", lambda v, rst=rst: v.reciprocal(out=rst[:], in_=rst[:]),
             reads=[rn], writes=[rn])
        p.op("vector", lambda v, mvt=mvt, rst=rst, ngt=ngt: v.tensor_scalar(out=ngt[:], in0=mvt[:, 0:1],
             scalar1=rst[:], scalar2=-1.0, op0=OP.mult, op1=OP.mult),
             reads=[mn, rn], writes=[nn])
        p.op("vector", lambda v, xtile=xtile, outtile=outtile, outoff=outoff, rst=rst, ngt=ngt:
             v.tensor_scalar(out=outtile[:, outoff:outoff + D], in0=xtile[:],
             scalar1=rst[:], scalar2=ngt[:], op0=OP.mult, op1=OP.add),
             reads=[xname, rn, nn], writes=[outname])

    rg = [[0, 1, 2, 3], [4, 5, 6, 7]]
    for half, (lnc, lncname, lnfull, lnfullname) in enumerate(
            [(lncA, "lncA", lnfullA, "lnfullA"), (lncB, "lncB", lnfullB, "lnfullB")]):
        for tt in range(4 * half, 4 * half + 4):
            xb = tt % 2
            p.op("sync", lambda s, tt=tt, xb=xb: s.dma_start(
                out=xt[xb][:], in_=xq[tt * 128:(tt + 1) * 128, :]),
                writes=[f"xt{xb}"], is_dma=True)
            emit_ln(xt[xb], f"xt{xb}", ln1tm, f"ln1tm{tt}", tt * D, par=xb)
        tnames = [f"ln1tm{t_}" for t_ in range(4 * half, 4 * half + 4)]
        for fk4 in range(8):
            pj = fk4 % 2
            pb = pAb if pj == 0 else pBb
            pnames = ["pA0", "pA1"] if pj == 0 else ["pB0", "pB1"]
            for fi in range(4):
                fk = fk4 * 4 + fi
                for ti in range(4):
                    tt = 4 * half + ti
                    p.op("tensor", lambda t, fk=fk, tt=tt, ti=ti, fi=fi, pb=pb: t.transpose(
                        out=pb[:, fi * 512 + ti * 128: fi * 512 + (ti + 1) * 128],
                        in_=ln1tm[:, tt * D + fk * 128: tt * D + (fk + 1) * 128],
                        identity=identb[:]),
                        reads=[tnames[ti], "identb"], writes=pnames[fi // 2:fi // 2 + 1])
            sb = fk4 % 2
            p.op("scalar", lambda s, pb=pb, sb=sb: s.copy(
                out=lnstage[sb][:], in_=pb[:, 0:2048]),
                reads=pnames, writes=[f"lnstage{sb}"])
            p.op("scalar", lambda s, fk4=fk4, sb=sb, lnc=lnc, lncname=lncname: s.dma_start(
                out=lnc.ap()[fk4 * 512:(fk4 + 1) * 512, :].rearrange("(a p) t -> p a t", p=128),
                in_=lnstage[sb][:].rearrange("p (a t) -> p a t", a=4)),
                reads=[f"lnstage{sb}"], writes=[f"{lncname}_{fk4}"], is_dma=True)
        p.op("gpsimd", lambda g, lnc=lnc, lnfull=lnfull: g.collective_compute(
            "AllGather", mybir.AluOpType.bypass, replica_groups=rg,
            ins=[lnc.ap().opt()], outs=[lnfull.ap().opt()]),
            reads=[f"{lncname}_{fk4}" for fk4 in range(8)], writes=[lnfullname], is_cc=True)

    p.fence()
    cur = [D0]
    wqb = alloc("wqb", [128, 8 * 32 * 128], BF16, 65536)
    lnb = [alloc(f"lnb{i}", [128, 32 * 512], BF16, 32768) for i in range(2)]
    wkb = alloc("wkb", [128, 2 * 32 * 128], BF16, 16384)
    wvb = alloc("wvb", [128, 32 * 256], BF16, 16384)
    rc = [alloc(f"rc{i}", [128, 512], F32, 2048) for i in range(2)]
    rs_ = [alloc(f"rs{i}", [128, 512], F32, 2048) for i in range(2)]
    rt1 = alloc("rt1", [128, 512], F32, 2048)
    rt2 = alloc("rt2", [128, 512], F32, 2048)
    qst = [alloc(f"qst{i}", [128, 512], BF16, 1024) for i in range(2)]

    p.op("gpsimd", lambda g: g.dma_start(out=wkb[:], in_=wk[:]), writes=["wkb"], is_dma=True)
    p.op("gpsimd", lambda g: g.dma_start(out=wvb[:], in_=wv[:]), writes=["wvb"], is_dma=True)
    p.op("gpsimd", lambda g: g.dma_start(out=wqb[:], in_=wq[:]), writes=["wqb"], is_dma=True)

    sr_list = [(r, 0) for r in range(4)] + [(r, 1) for r in range(4)]

    def emit_rope(psname, ps, coff, outname, outtile, outoff, srt0):
        p.op("vector", lambda v, ps=ps, coff=coff, srt0=srt0: v.tensor_tensor(
            out=rt1[:], in0=ps[:, coff:coff + 512], in1=rc[srt0][:], op=OP.mult),
            reads=[psname, f"rc{srt0}"], writes=["rt1"])
        p.op("vector", lambda v, ps=ps, coff=coff, srt0=srt0: v.tensor_tensor(
            out=rt2[0:64, :], in0=ps[64:128, coff:coff + 512], in1=rs_[srt0][0:64, :], op=OP.mult),
            reads=[psname, f"rs{srt0}"], writes=["rt2"])
        p.op("vector", lambda v, ps=ps, coff=coff, srt0=srt0: v.tensor_tensor(
            out=rt2[64:128, :], in0=ps[0:64, coff:coff + 512], in1=rs_[srt0][64:128, :], op=OP.mult),
            reads=[psname, f"rs{srt0}"], writes=["rt2"])
        p.op("vector", lambda v, outtile=outtile, outoff=outoff: v.tensor_tensor(
            out=outtile[:, outoff:outoff + 512], in0=rt1[:], in1=rt2[:], op=OP.add),
            reads=["rt1", "rt2"], writes=[outname])

    for si, (rb, th) in enumerate(sr_list):
        t0 = rb * 1024 + th * 512
        lb = si % 2
        lnfull = lnfullA if th == 0 else lnfullB
        lname = "lnfullA" if th == 0 else "lnfullB"
        srb = si % 2
        p.op("sync", lambda s, lb=lb, lnfull=lnfull, rb=rb: s.dma_start(
            out=lnb[lb][:].rearrange("p (a t) -> p a t", a=32),
            in_=lnfull.ap()[rb * D:(rb + 1) * D, :].rearrange("(a p) t -> p a t", p=128)),
            reads=[lname], writes=[f"lnb{lb}"], is_dma=True)
        p.op("sync", lambda s, srb=srb, t0=t0: s.dma_start(
            out=rc[srb][:], in_=ropec[:, t0:t0 + 512]), writes=[f"rc{srb}"], is_dma=True)
        p.op("sync", lambda s, srb=srb, t0=t0: s.dma_start(
            out=rs_[srb][:], in_=ropes[:, t0:t0 + 512]), writes=[f"rs{srb}"], is_dma=True)

        for kc in range(2):
            pj = kc % 2
            for fk in range(32):
                p.op("tensor", lambda t, kc=kc, fk=fk, lb=lb, pj=pj: t.matmul(
                    pC[pj][:], wkb[:, (kc * 32 + fk) * 128:(kc * 32 + fk + 1) * 128],
                    lnb[lb][:, fk * 512:(fk + 1) * 512],
                    start=(fk == 0), stop=(fk == 31)),
                    reads=["wkb", f"lnb{lb}"], writes=[f"pC{pj}"])
            emit_rope(f"pC{pj}", pC[pj], 0, "K", K, kc * T + t0, srb)
        for ttl in range(4):
            pj = 2 + ttl % 2
            for fk in range(32):
                p.op("tensor", lambda t, ttl=ttl, fk=fk, lb=lb, pj=pj: t.matmul(
                    pC[pj][:, 0:256],
                    lnb[lb][:, fk * 512 + ttl * 128: fk * 512 + (ttl + 1) * 128],
                    wvb[:, fk * 256:(fk + 1) * 256],
                    start=(fk == 0), stop=(fk == 31)),
                    reads=["wvb", f"lnb{lb}"], writes=[f"pC{pj}"])
            gtt = (t0 // 128) + ttl
            p.op("vector", lambda v, pj=pj, gtt=gtt: v.tensor_copy(
                out=V[:, gtt * 256:(gtt + 1) * 256], in_=pC[pj][:, 0:256]),
                reads=[f"pC{pj}"], writes=["V"])
        for cc in range(8):
            pj = cc % 2
            for fk in range(32):
                p.op("tensor", lambda t, cc=cc, fk=fk, lb=lb, pj=pj: t.matmul(
                    pC[pj][:], wqb[:, (cc * 32 + fk) * 128:(cc * 32 + fk + 1) * 128],
                    lnb[lb][:, fk * 512:(fk + 1) * 512],
                    start=(fk == 0), stop=(fk == 31)),
                    reads=["wqb", f"lnb{lb}"], writes=[f"pC{pj}"])
            qb = cc % 2
            emit_rope(f"pC{pj}", pC[pj], 0, f"qst{qb}", qst[qb], 0, srb)
            p.op("scalar", lambda s, qb=qb: s.copy(out=qst[qb][:], in_=qst[qb][:]),
                 reads=[f"qst{qb}"], writes=[f"qst{qb}"])
            p.op("scalar", lambda s, cc=cc, t0=t0, qb=qb: s.dma_start(
                out=qspill[cc * 128:(cc + 1) * 128, t0:t0 + 512], in_=qst[qb][:]),
                reads=[f"qst{qb}"], writes=[f"qspill_{cc}_{t0}"], is_dma=True)

    p.fence()
    cur = [D0]
    Y = alloc("Y", [128, NQH * T], BF16, 65536)
    qbuf = [alloc(f"qbuf{i}", [128, T], BF16, 8192) for i in range(2)]
    NPB = 4
    Pb = [alloc(f"P{i}", [128, 1024], BF16, 2048) for i in range(NPB)]
    maskb = alloc("maskb", [128, 2048], F32, 8192)
    recb = [alloc(f"rec{i}", [128, 512], F32, 2048) for i in range(2)]

    p.op("sync", lambda s: s.dma_start(out=maskb[:], in_=maskin[:]),
         writes=["maskb"], is_dma=True)
    p.op("scalar", lambda s: s.dma_start(out=qbuf[0][:], in_=qspill[0:128, :]),
         reads=[f"qspill_0_{t0}" for t0 in range(0, T, 512)], writes=["qbuf0"], is_dma=True)

    batches = []
    for qh in range(NQH):
        for qc in range(NSR):
            for sc in range(qc + 1):
                for half in range(2):
                    batches.append((qh, qc, sc, half,
                                    sc == 0 and half == 0,
                                    sc == qc and half == 1))

    def att_qk(i):
        qh, qc, sc, half, first, last = batches[i]
        kv = qh // 4
        pj = i % 2
        pqk = pA if pj == 0 else pB
        pn = ["pA0", "pA1"] if pj == 0 else ["pB0", "pB1"]
        qb = qh % 2
        if qc == 2 and sc == 0 and half == 0 and qh + 1 < NQH:
            p.op("scalar", lambda s, qh=qh: s.dma_start(
                out=qbuf[(qh + 1) % 2][:],
                in_=qspill[(qh + 1) * 128:(qh + 2) * 128, :]),
                reads=[f"qspill_{qh + 1}_{t0}" for t0 in range(0, T, 512)],
                writes=[f"qbuf{(qh + 1) % 2}"], is_dma=True)
        for k2 in range(2):
            sub = sc * 4 + half * 2 + k2
            p.op("tensor", lambda t, sub=sub, k2=k2, pqk=pqk, qc=qc, kv=kv, qb=qb: t.matmul(
                pqk[:, k2 * 512:(k2 + 1) * 512],
                K[:, kv * T + sub * 128: kv * T + (sub + 1) * 128],
                qbuf[qb][:, qc * 512:(qc + 1) * 512],
                start=True, stop=True),
                reads=["K", f"qbuf{qb}"], writes=[pn[k2]])

    def att_exp(i):
        qh, qc, sc, half, first, last = batches[i]
        pj = i % 2
        pqk = pA if pj == 0 else pB
        pn = ["pA0", "pA1"] if pj == 0 else ["pB0", "pB1"]
        pp = i % NPB
        p.op("scalar", lambda s, pqk=pqk, pp=pp: s.activation(
            out=Pb[pp][:], in_=pqk[:], func=AF.Exp, bias=biasc[:], scale=SCALING),
            reads=pn + ["biasc"], writes=[f"P{pp}"])
        if sc == qc:
            p.op("vector", lambda v, pp=pp, half=half: v.tensor_tensor(
                out=Pb[pp][:], in0=Pb[pp][:],
                in1=maskb[:, half * 1024:(half + 1) * 1024], op=OP.mult),
                reads=[f"P{pp}", "maskb"], writes=[f"P{pp}"])

    def att_av(i):
        qh, qc, sc, half, first, last = batches[i]
        kv = qh // 4
        pp = i % NPB
        ypj = qc % 2
        for k2 in range(2):
            sub = sc * 4 + half * 2 + k2
            p.op("tensor", lambda t, sub=sub, pp=pp, ypj=ypj, first=first, last=last, k2=k2, kv=kv: t.matmul(
                pC[ypj][:], V[:, sub * 256 + kv * 128: sub * 256 + (kv + 1) * 128],
                Pb[pp][:, k2 * 512:(k2 + 1) * 512],
                start=(first and k2 == 0), stop=(last and k2 == 1)),
                reads=[f"P{pp}", "V"], writes=[f"pC{ypj}"])
            p.op("tensor", lambda t, pp=pp, ypj=ypj, first=first, last=last, k2=k2: t.matmul(
                pC[2 + ypj][:], ones[:], Pb[pp][:, k2 * 512:(k2 + 1) * 512],
                start=(first and k2 == 0), stop=(last and k2 == 1)),
                reads=[f"P{pp}", "ones"], writes=[f"pC{2 + ypj}"])
        if last:
            p.op("vector", lambda v, ypj=ypj: v.reciprocal(
                out=recb[ypj][:], in_=pC[2 + ypj][:]),
                reads=[f"pC{2 + ypj}"], writes=[f"rec{ypj}"])
            p.op("vector", lambda v, ypj=ypj, qh=qh, qc=qc: v.tensor_tensor(
                out=Y[:, qh * T + qc * 512: qh * T + (qc + 1) * 512],
                in0=pC[ypj][:], in1=recb[ypj][:], op=OP.mult),
                reads=[f"pC{ypj}", f"rec{ypj}"], writes=["Y"])

    att_qk(0)
    att_exp(0)
    att_qk(1)
    for i in range(len(batches)):
        if i + 1 < len(batches):
            att_exp(i + 1)
        if i + 2 < len(batches):
            att_qk(i + 2)
        att_av(i)

    p.fence()
    cur = [D0 + 65536]
    wob = alloc("wob", [128, NCK * 8 * 512], BF16, NCK * 8192)
    ost = [alloc(f"ost{i}", [128, 2048], F32, 8192) for i in range(4)]

    p.op("gpsimd", lambda g: g.dma_start(
        out=wob[:, 0:4096], in_=wo[:, 0, :]), writes=["wob0"], is_dma=True)
    for ck in range(NCK):
        for tg in range(8):
            if tg == 1 and ck + 1 < NCK:
                p.op("gpsimd", lambda g, ck=ck: g.dma_start(
                    out=wob[:, (ck + 1) * 4096:(ck + 2) * 4096], in_=wo[:, ck + 1, :]),
                    writes=[f"wob{ck + 1}"], is_dma=True)
            sj = tg % 4
            for ti in range(4):
                tt = tg * 4 + ti
                slot = (tg * 4 + ti) % 8
                if slot < 4:
                    pt, ptn = pC[slot][:], f"pC{slot}"
                else:
                    base = pA if slot < 6 else pB
                    off = (slot % 2) * 512
                    pt = base[:, off:off + 512]
                    ptn = ["pA0", "pA1", "pB0", "pB1"][slot - 4]
                for fk in range(8):
                    p.op("tensor", lambda t, tt=tt, fk=fk, ck=ck, pt=pt: t.matmul(
                        pt, Y[:, fk * T + tt * 128: fk * T + (tt + 1) * 128],
                        wob[:, (ck * 8 + fk) * 512:(ck * 8 + fk + 1) * 512],
                        start=(fk == 0), stop=(fk == 7)),
                        reads=["Y", f"wob{ck}"], writes=[ptn])
                p.op("scalar", lambda s, ti=ti, sj=sj, pt=pt: s.copy(
                    out=ost[sj][:, ti * 512:(ti + 1) * 512], in_=pt),
                    reads=[ptn], writes=[f"ost{sj}"])
            p.op("scalar", lambda s, ck=ck, tg=tg, sj=sj: s.dma_start(
                out=pck[ck][tg], in_=ost[sj][:]),
                reads=[f"ost{sj}"], writes=[f"pck{ck}_{tg}"], is_dma=True)
        p.op("gpsimd", lambda g, ck=ck: g.collective_compute(
            "ReduceScatter", mybir.AluOpType.add, replica_groups=rg,
            ins=[pck[ck].ap().opt()], outs=[rck[ck].ap().opt()]),
            reads=[f"pck{ck}_{tg}" for tg in range(8)], writes=[f"rck{ck}"], is_cc=True)

    p.fence()
    cur = [D0]
    fxt = [alloc(f"fxt{i}", [128, D], F32, 16384) for i in range(2)]
    fat = [alloc(f"fat{i}", [128, D], F32, 16384) for i in range(2)]
    x2t = [alloc(f"x2t{i}", [128, D], F32, 16384) for i in range(2)]
    ln2tm = [alloc(f"ln2tm{i}", [128, D], BF16, 8192) for i in range(2)]
    ln2fm = [alloc(f"ln2fm{i}", [128, 32 * 128], BF16, 8192) for i in range(2)]
    awrb = alloc("awrb", [128, 32 * 132], BF16, 8448)
    bstb = alloc("bstb", [128, 4096], BF16, 8192)
    outst = fxt
    lgt = [alloc(f"lgt{i}", [128, 4], F32, 64) for i in range(2)]
    m1t = [alloc(f"m1t{i}", [128, 1], F32, 64) for i in range(2)]
    nm1t = [alloc(f"nm1t{i}", [128, 1], F32, 64) for i in range(2)]
    l2t = [alloc(f"l2t{i}", [128, 4], F32, 64) for i in range(2)]
    m2t = [alloc(f"m2t{i}", [128, 1], F32, 64) for i in range(2)]
    e2t = [alloc(f"e2t{i}", [128, 1], F32, 64) for i in range(2)]
    rdt = [alloc(f"rdt{i}", [128, 1], F32, 64) for i in range(2)]
    ge1t = [alloc(f"ge1t{i}", [128, 4], F32, 64) for i in range(2)]
    ge2t = [alloc(f"ge2t{i}", [128, 4], F32, 64) for i in range(2)]
    gnt = [alloc(f"gnt{i}", [128, 4], F32, 64) for i in range(2)]
    gtt_ = [alloc(f"gt{i}", [128, 4], F32, 64) for i in range(2)]
    midt = [alloc(f"midt{i}", [128, 128], BF16, 256) for i in range(2)]
    midf = [alloc(f"midf{i}", [128, 128], BF16, 256) for i in range(2)]

    p.op("gpsimd", lambda g: g.dma_start(out=awrb[:], in_=awr[:]), writes=["awrb"], is_dma=True)
    p.op("gpsimd", lambda g: g.dma_start(out=bstb[:], in_=bst[:]), writes=["bstb"], is_dma=True)

    for tt in range(NTT):
        db = tt % 2
        p.op("sync", lambda s, tt=tt, db=db: s.dma_start(
            out=fxt[db][:], in_=xq[tt * 128:(tt + 1) * 128, :]),
            writes=[f"fxt{db}"], is_dma=True)
        for ck in range(NCK):
            p.op("scalar", lambda s, tt=tt, ck=ck, db=db: s.dma_start(
                out=fat[db][:, ck * 512:(ck + 1) * 512],
                in_=rck[ck][tt // 4, :, (tt % 4) * 512:(tt % 4 + 1) * 512]),
                reads=[f"rck{ck}"], writes=[f"fat{db}_{ck}"], is_dma=True)
        p.op("vector", lambda v, db=db: v.tensor_tensor(
            out=x2t[db][:], in0=fxt[db][:], in1=fat[db][:], op=OP.add),
            reads=[f"fxt{db}"] + [f"fat{db}_{ck}" for ck in range(NCK)],
            writes=[f"x2t{db}"])
        emit_ln(x2t[db], f"x2t{db}", ln2tm[db], f"ln2tm{db}", 0, par=db)
        for g8 in range(4):
            pj = (g8 + db) % 2
            pb = pAb if pj == 0 else pBb
            pn = ["pA0"] if pj == 0 else ["pB0"]
            for j in range(8):
                p.op("tensor", lambda t, g8=g8, j=j, pb=pb, db=db: t.transpose(
                    out=pb[:, j * 128:(j + 1) * 128],
                    in_=ln2tm[db][:, (g8 * 8 + j) * 128:(g8 * 8 + j + 1) * 128],
                    identity=identb[:]),
                    reads=[f"ln2tm{db}", "identb"], writes=pn)
            p.op("vector", lambda v, g8=g8, pb=pb, db=db: v.tensor_copy(
                out=ln2fm[db][:, g8 * 1024:(g8 + 1) * 1024], in_=pb[:, 0:1024]),
                reads=pn, writes=[f"ln2fm{db}"])
        pm = db
        pmn = f"pC{pm}"
        _lg, _m1, _nm1, _l2, _m2, _e2 = lgt[db], m1t[db], nm1t[db], l2t[db], m2t[db], e2t[db]
        _rd, _g1, _g2, _gn, _gt = rdt[db], ge1t[db], ge2t[db], gnt[db], gtt_[db]
        _mt, _mf = midt[db], midf[db]
        lgn, m1n, nm1n, l2n, m2n, e2n = (f"lgt{db}", f"m1t{db}", f"nm1t{db}",
                                         f"l2t{db}", f"m2t{db}", f"e2t{db}")
        rdn, g1n, g2n, gnn, gtn = (f"rdt{db}", f"ge1t{db}", f"ge2t{db}",
                                   f"gnt{db}", f"gt{db}")
        mtn, mfn = f"midt{db}", f"midf{db}"
        for fk in range(32):
            p.op("tensor", lambda t, fk=fk, pm=pm, db=db: t.matmul(
                pC[pm][:, 0:132], ln2fm[db][:, fk * 128:(fk + 1) * 128],
                awrb[:, fk * 132:(fk + 1) * 132],
                start=(fk == 0), stop=(fk == 31)),
                reads=[f"ln2fm{db}", "awrb"], writes=[pmn])
        p.op("vector", lambda v, pm=pm, _lg=_lg: v.tensor_copy(out=_lg[:], in_=pC[pm][:, 128:132]),
             reads=[pmn], writes=[lgn])
        p.op("vector", lambda v, _lg=_lg, _m1=_m1: v.reduce_max(out=_m1[:], in_=_lg[:], axis=AX.X),
             reads=[lgn], writes=[m1n])
        p.op("vector", lambda v, _m1=_m1, _nm1=_nm1: v.tensor_scalar(out=_nm1[:], in0=_m1[:],
             scalar1=-1.0, scalar2=None, op0=OP.mult), reads=[m1n], writes=[nm1n])
        p.op("vector", lambda v, _lg=_lg, _m1=_m1, _g1=_g1: v.tensor_scalar(out=_g1[:], in0=_lg[:],
             scalar1=_m1[:], scalar2=None, op0=OP.is_ge), reads=[lgn, m1n], writes=[g1n])
        p.op("vector", lambda v, _g1=_g1, _lg=_lg, _l2=_l2: v.scalar_tensor_tensor(out=_l2[:], in0=_g1[:],
             scalar=-1e30, in1=_lg[:], op0=OP.mult, op1=OP.add),
             reads=[g1n, lgn], writes=[l2n])
        p.op("vector", lambda v, _l2=_l2, _m2=_m2: v.reduce_max(out=_m2[:], in_=_l2[:], axis=AX.X),
             reads=[l2n], writes=[m2n])
        p.op("scalar", lambda s, _m2=_m2, _e2=_e2, _nm1=_nm1: s.activation(out=_e2[:], in_=_m2[:], func=AF.Exp,
             bias=_nm1[:]), reads=[m2n, nm1n], writes=[e2n])
        p.op("vector", lambda v, _e2=_e2, _rd=_rd: v.tensor_scalar(out=_rd[:], in0=_e2[:],
             scalar1=1.0, scalar2=None, op0=OP.add), reads=[e2n], writes=[rdn])
        p.op("vector", lambda v, _rd=_rd: v.reciprocal(out=_rd[:], in_=_rd[:]),
             reads=[rdn], writes=[rdn])
        p.op("scalar", lambda s, _lg=_lg, _gn=_gn, _nm1=_nm1: s.activation(out=_gn[:], in_=_lg[:],
             func=AF.Exp, bias=_nm1[:]), reads=[lgn, nm1n], writes=[gnn])
        p.op("vector", lambda v, _lg=_lg, _m2=_m2, _g2=_g2: v.tensor_scalar(out=_g2[:], in0=_lg[:],
             scalar1=_m2[:], scalar2=None, op0=OP.is_ge), reads=[lgn, m2n], writes=[g2n])
        p.op("vector", lambda v, _gn=_gn, _g2=_g2: v.tensor_tensor(out=_gn[:], in0=_gn[:], in1=_g2[:],
             op=OP.mult), reads=[gnn, g2n], writes=[gnn])
        p.op("vector", lambda v, _gn=_gn, _rd=_rd, _gt=_gt: v.tensor_scalar(out=_gt[:], in0=_gn[:],
             scalar1=_rd[:], scalar2=4.0, op0=OP.mult, op1=OP.mult),
             reads=[gnn, rdn], writes=[gtn])
        for e in range(4):
            p.op("vector", lambda v, e=e, pm=pm, _gt=_gt, _mt=_mt: v.tensor_scalar(
                out=_mt[:, e * 32:(e + 1) * 32], in0=pC[pm][:, e * 32:(e + 1) * 32],
                scalar1=_gt[:, e:e + 1], scalar2=None, op0=OP.mult),
                reads=[pmn, gtn], writes=[mtn])
        trp = pAb if db == 0 else pBb
        trn = "pA0" if db == 0 else "pB0"
        p.op("tensor", lambda t, trp=trp, _mt=_mt: t.transpose(out=trp[:, 0:128], in_=_mt[:],
             identity=identb[:]), reads=[mtn, "identb"], writes=[trn])
        p.op("vector", lambda v, trp=trp, _mf=_mf: v.tensor_copy(out=_mf[:], in_=trp[:, 0:128]),
             reads=[trn], writes=[mfn])
        for ck in range(NCK):
            pj = 2 + ck % 2
            p.op("tensor", lambda t, ck=ck, pj=pj, _mf=_mf: t.matmul(
                pC[pj][:], _mf[:], bstb[:, ck * 512:(ck + 1) * 512],
                start=True, stop=True), reads=[mfn, "bstb"], writes=[f"pC{pj}"])
            p.op("vector", lambda v, ck=ck, pj=pj, db=db: v.tensor_tensor(
                out=outst[db][:, ck * 512:(ck + 1) * 512], in0=pC[pj][:],
                in1=x2t[db][:, ck * 512:(ck + 1) * 512], op=OP.add),
                reads=[f"pC{pj}", f"x2t{db}"], writes=[f"fxt{db}"])
        p.op("scalar", lambda s, db=db: s.copy(out=outst[db][:], in_=outst[db][:]),
             reads=[f"fxt{db}"], writes=[f"fxt{db}"])
        p.op("sync", lambda s, tt=tt, db=db: s.dma_start(
            out=out[tt * 128:(tt + 1) * 128, :], in_=outst[db][:]),
            reads=[f"fxt{db}"], writes=["outdram"], is_dma=True)

    stack2 = p.build(end_engine="sync")
    return nc, (st, stack2)


def _host_prep(x, w_qkv, w_o, A, Bm, Wr):
    w_qkv = np.asarray(w_qkv, dtype=np.float32)
    w_o = np.asarray(w_o, dtype=np.float32)
    A = np.asarray(A, dtype=np.float32)
    Bm = np.asarray(Bm, dtype=np.float32)
    Wr = np.asarray(Wr, dtype=np.float32)
    x = np.asarray(x, dtype=np.float32)

    inv = 1.0 / (10000.0 ** (np.arange(0, HD, 2, dtype=np.float32) / HD))
    pos = np.arange(T, dtype=np.float32)
    freqs = np.outer(pos, inv)
    emb = np.concatenate([freqs, freqs], axis=1)
    cosT = np.cos(emb).T.astype(np.float32).copy()
    sinT = np.sin(emb).T.astype(np.float32)
    sinT = sinT.copy()
    sinT[0:64, :] *= -1.0
    mask = np.zeros((128, 2048), dtype=np.float32)
    qidx = np.arange(512)
    for k in range(4):
        for s in range(128):
            mask[s, k * 512:(k + 1) * 512] = (k * 128 + s) <= qidx
    ident = np.eye(128, dtype=np.float32)
    AWr = np.zeros((D, 132), dtype=np.float32)
    for e in range(4):
        AWr[:, e * 32:e * 32 + 16] = A[e].T
        AWr[:, 128 + e] = Wr[e]
    awr_h = np.ascontiguousarray(AWr.reshape(32, 128, 132).transpose(1, 0, 2)
                                 ).reshape(128, 32 * 132)
    bstk = np.zeros((128, D), dtype=np.float32)
    for e in range(4):
        bstk[e * 32:e * 32 + 16, :] = Bm[e].T
    in_maps = []
    for c in range(8):
        b, r = c // 4, c % 4
        wq_h = np.ascontiguousarray(
            w_qkv[:, 1024 * r:1024 * (r + 1)].reshape(32, 128, 8, 128)
            .transpose(1, 2, 0, 3)).reshape(128, 8 * 32 * 128)
        wk_h = np.ascontiguousarray(
            w_qkv[:, 4096 + 256 * r:4096 + 256 * (r + 1)].reshape(32, 128, 2, 128)
            .transpose(1, 2, 0, 3)).reshape(128, 2 * 32 * 128)
        wv_h = np.ascontiguousarray(
            w_qkv[:, 5120 + 256 * r:5120 + 256 * (r + 1)].reshape(32, 128, 256)
            .transpose(1, 0, 2)).reshape(128, 32 * 256)
        wo_h = np.ascontiguousarray(
            w_o[1024 * r:1024 * (r + 1), :].reshape(8, 128, 8, 512)
            .transpose(1, 2, 0, 3)).reshape(128, 8, 8 * 512)
        in_maps.append({
            "xq": np.ascontiguousarray(x[b, r * 1024:(r + 1) * 1024, :]),
            "wq": wq_h, "wk": wk_h, "wv": wv_h, "wo": wo_h,
            "awr": awr_h, "bst": bstk,
            "ropec": cosT, "ropes": sinT, "maskin": mask, "identin": ident,
        })
    return in_maps


def kernel(L0_attn_hidden_states, L0_attn_w_qkv, L0_attn_w_o,
           L0_ffn_A, L0_ffn_B, L0_ffn_Wr):
    in_maps = _host_prep(L0_attn_hidden_states, L0_attn_w_qkv, L0_attn_w_o,
                         L0_ffn_A, L0_ffn_B, L0_ffn_Wr)
    nc, stacks = build_nc()
    res = run_bass_kernel_spmd(nc, in_maps, core_ids=list(range(8)))
    outp = np.empty((B, T, D), dtype=np.float32)
    for c in range(8):
        b, r = c // 4, c % 4
        outp[b, r * 1024:(r + 1) * 1024, :] = res.results[c]["out"]
    return outp


if __name__ == "__main__":
    import reference as _ref
    inputs = _ref.setup_inputs()
    got = kernel(**{k: np.asarray(v) for k, v in inputs.items()})
    exp = np.asarray(_ref.reference(**inputs))
    num = float(np.mean((got - exp) ** 2))
    den = float(np.mean(exp ** 2))
    print("residual variance ratio:", num / den)


# version 30
# speedup vs baseline: 1.0280x; 1.0280x over previous
import contextlib

import numpy as np
import concourse.bass as bass
import concourse.mybir as mybir
from concourse.bass_utils import run_bass_kernel_spmd
from concourse.alu_op_type import AluOpType as OP

F32 = mybir.dt.float32
BF16 = mybir.dt.bfloat16
AF = mybir.ActivationFunctionType
AX = mybir.AxisListType

B, T, D = 2, 4096, 4096
NH, NKV, HD = 32, 8, 128
SCALING = HD ** -0.5
CSH = -14.0
EPS = 1e-5
TQ = 1024
NTT = TQ // 128
NSR = T // 512
NQH = 8
NCK = 8

ENGINES = ("tensor", "vector", "scalar", "gpsimd", "sync")


class Prog:

    def __init__(self, nc):
        self.nc = nc
        self.ops = []
        self.eng_count = {e: 0 for e in ENGINES}
        self.dma_count = {e: 0 for e in ENGINES}
        self.cc_count = 0
        self.hwm = {e: {} for e in ENGINES}
        self.bufs = {}
        self._fence = {}

    def buf(self, name):
        if name not in self.bufs:
            self.bufs[name] = [None, []]
        return self.bufs[name]

    def fence(self):
        for e in ENGINES:
            if self.eng_count[e]:
                self._fence["e_" + e] = self.eng_count[e]
            if self.dma_count[e]:
                self._fence["q_" + e] = self.dma_count[e] * 16
        if self.cc_count:
            self._fence["cc"] = self.cc_count
        self._fence = dict(self._fence)

    def op(self, engine, fn, reads=(), writes=(), is_dma=False, is_cc=False,
           extra_waits=()):
        waits = {}

        def add(tok, waw_self=False):
            if tok is None:
                return
            s, c = tok
            if s.startswith("q_"):
                qeng = s[2:]
                if waw_self and is_dma and qeng == engine:
                    return
                c = 16 * self.dma_count[qeng]
                if c == 0:
                    return
            if waits.get(s, 0) < c:
                waits[s] = c

        for b in reads:
            add(self.buf(b)[0])
        for b in writes:
            bb = self.buf(b)
            add(bb[0], waw_self=True)
            for t in bb[1]:
                add(t)
        for t in extra_waits:
            add(t)
        for s, c in self._fence.items():
            add((s, c))
        hw = self.hwm[engine]
        final = []
        for s, c in waits.items():
            if engine == "tensor" and s == "e_tensor":
                continue
            if hw.get(s, 0) >= c:
                continue
            final.append((s, c))
            hw[s] = c
        if is_cc:
            tok = ("cc", self.cc_count + 1)
            self.cc_count += 1
        elif is_dma:
            tok = ("q_" + engine, (self.dma_count[engine] + 1) * 16)
            self.dma_count[engine] += 1
        else:
            tok = ("e_" + engine, self.eng_count[engine] + 1)
            self.eng_count[engine] += 1
        self.ops.append((engine, fn, final, tok, is_dma or is_cc))
        for b in writes:
            bb = self.buf(b)
            bb[0] = tok
            bb[1] = []
        for b in reads:
            self.buf(b)[1].append(tok)
        return tok

    def build(self, end_engine="sync"):
        nc = self.nc
        stack = contextlib.ExitStack()
        sems = {}
        for e in ENGINES:
            if self.eng_count[e]:
                sems["e_" + e] = stack.enter_context(nc.semaphore("e_" + e))
            if self.dma_count[e] or e == end_engine:
                sems["q_" + e] = stack.enter_context(nc.semaphore("q_" + e))
        if self.cc_count:
            sems["cc"] = stack.enter_context(nc.semaphore("cc"))
        finals = {}
        for e in ENGINES:
            if self.eng_count[e]:
                finals["e_" + e] = self.eng_count[e]
            if self.dma_count[e]:
                finals["q_" + e] = self.dma_count[e] * 16
        if self.cc_count:
            finals["cc"] = self.cc_count
        per_eng = {e: [] for e in ENGINES}
        for rec in self.ops:
            per_eng[rec[0]].append(rec)
        block = stack.enter_context(nc.Block())

        def make_body(eng_name, recs):
            def body(eng):
                for _, fn, fwaits, tok, is_dma in recs:
                    for s, c in fwaits:
                        eng.wait_ge(sems[s], c)
                    instr = fn(eng)
                    instr.then_inc(sems[tok[0]],
                                   16 if (is_dma and tok[0].startswith("q_")) else 1)
                if eng_name == end_engine:
                    for s, c in finals.items():
                        if s == "e_" + end_engine:
                            continue
                    for s, c in finals.items():
                        if s == "e_" + end_engine:
                            continue
                        eng.wait_ge(sems[s], c)
            return body

        for e in ENGINES:
            if per_eng[e] or e == end_engine:
                getattr(block, e)(make_body(e, per_eng[e]))
        return stack


def build_nc():
    nc = bass.Bass("TRN2", target_bir_lowering=False, debug=False)

    xq = nc.dram_tensor("xq", [TQ, D], F32, kind="ExternalInput").ap()
    wq = nc.dram_tensor("wq", [128, 8 * 32 * 128], F32, kind="ExternalInput").ap()
    wk = nc.dram_tensor("wk", [128, 2 * 32 * 128], F32, kind="ExternalInput").ap()
    wv = nc.dram_tensor("wv", [128, 32 * 256], F32, kind="ExternalInput").ap()
    wo = nc.dram_tensor("wo", [128, 8, 8 * 512], F32, kind="ExternalInput").ap()
    awr = nc.dram_tensor("awr", [128, 32 * 132], F32, kind="ExternalInput").ap()
    bst = nc.dram_tensor("bst", [128, 4096], F32, kind="ExternalInput").ap()
    ropec = nc.dram_tensor("ropec", [128, T], F32, kind="ExternalInput").ap()
    ropes = nc.dram_tensor("ropes", [128, T], F32, kind="ExternalInput").ap()
    maskin = nc.dram_tensor("maskin", [128, 2048], F32, kind="ExternalInput").ap()
    identin = nc.dram_tensor("identin", [128, 128], F32, kind="ExternalInput").ap()
    out = nc.dram_tensor("out", [TQ, D], F32, kind="ExternalOutput").ap()

    lnfullA = nc.dram_tensor("lnfullA", [4 * D, 512], BF16)
    lnfullB = nc.dram_tensor("lnfullB", [4 * D, 512], BF16)
    lncA = nc.dram_tensor("lncA", [D, 512], BF16)
    lncB = nc.dram_tensor("lncB", [D, 512], BF16)
    qspill = nc.dram_tensor("qspill", [NQH * 128, T], BF16)
    pck = [nc.dram_tensor(f"pck{k}", [8, 128, 2048], F32) for k in range(NCK)]
    rck = [nc.dram_tensor(f"rck{k}", [2, 128, 2048], F32) for k in range(NCK)]

    p = Prog(nc)
    st = contextlib.ExitStack()

    P0 = 16512
    def at(name, shape, dtype, off):
        return nc.alloc_sbuf_tensor_at(name, shape, dtype, offset=off)

    K = at("K", [128, 2 * T], BF16, P0)
    V = at("V", [128, 32 * 256], BF16, P0 + 16384)
    SM = P0 + 32768
    ones = at("ones", [128, 128], BF16, SM)
    biasc = at("biasc", [128, 1], F32, SM + 256)
    epst = at("epst", [128, 1], F32, SM + 288)
    stats = [at(f"stats{i}", [128, 48], F32, SM + 320 + 192 * i) for i in range(2)]
    mv = [at(f"mv{i}", [128, 2], F32, SM + 704 + 32 * i) for i in range(2)]
    rstd = [at(f"rstd{i}", [128, 1], F32, SM + 768 + 32 * i) for i in range(2)]
    negm = [at(f"negm{i}", [128, 1], F32, SM + 832 + 32 * i) for i in range(2)]
    identb = at("identb", [128, 128], BF16, SM + 896)
    D0 = SM + 1280
    DEND = 229300

    pA = st.enter_context(nc.psum_tensor("pA", [128, 1024], F32))
    pB = st.enter_context(nc.psum_tensor("pB", [128, 1024], F32))
    pC = [st.enter_context(nc.psum_tensor(f"pC{i}", [128, 512], F32)) for i in range(4)]
    pAb = pA[:].bitcast(BF16)
    pBb = pB[:].bitcast(BF16)

    p.op("vector", lambda v: v.memset(ones[:], 1.0), writes=["ones"])
    p.op("vector", lambda v: v.memset(biasc[:], CSH), writes=["biasc"])
    p.op("vector", lambda v: v.memset(epst[:], EPS), writes=["epst"])
    p.op("gpsimd", lambda g: g.dma_start(out=identb[:], in_=identin[:]),
         writes=["identb"], is_dma=True)

    cur = [D0]
    def alloc(name, shape, dtype, nbytes):
        nbytes = (nbytes + 31) // 32 * 32
        t = at(name, shape, dtype, cur[0])
        cur[0] += nbytes
        assert cur[0] <= DEND, f"SBUF overflow at {name}: {cur[0]}"
        return t

    xt = [alloc(f"xt{i}", [128, D], F32, 16384) for i in range(2)]
    ln1tm = alloc("ln1tm", [128, NTT * D], BF16, NTT * D * 2 // 1)
    lnstage = [alloc(f"lnstage{i}", [128, 2048], BF16, 4096) for i in range(2)]

    def emit_ln(xtile, xname, outtile, outname, outoff, par=0):
        stt, mvt, rst, ngt = stats[par], mv[par], rstd[par], negm[par]
        sn, mn, rn, nn = f"stats{par}", f"mv{par}", f"rstd{par}", f"negm{par}"
        for i in range(8):
            p.op("vector", lambda v, i=i, xtile=xtile, stt=stt: v.bn_stats(
                out=stt[:, i * 6:(i + 1) * 6], in_=xtile[:, i * 512:(i + 1) * 512]),
                reads=[xname], writes=[sn])
        p.op("vector", lambda v, stt=stt, mvt=mvt: v.bn_aggr(out=mvt[:], in_=stt[:]),
             reads=[sn], writes=[mn])
        p.op("scalar", lambda s, mvt=mvt, rst=rst: s.activation(out=rst[:], in_=mvt[:, 1:2],
             func=AF.Sqrt, bias=epst[:]), reads=[mn, "epst"], writes=[rn])
        p.op("vector", lambda v, rst=rst: v.reciprocal(out=rst[:], in_=rst[:]),
             reads=[rn], writes=[rn])
        p.op("vector", lambda v, mvt=mvt, rst=rst, ngt=ngt: v.tensor_scalar(out=ngt[:], in0=mvt[:, 0:1],
             scalar1=rst[:], scalar2=-1.0, op0=OP.mult, op1=OP.mult),
             reads=[mn, rn], writes=[nn])
        p.op("vector", lambda v, xtile=xtile, outtile=outtile, outoff=outoff, rst=rst, ngt=ngt:
             v.tensor_scalar(out=outtile[:, outoff:outoff + D], in0=xtile[:],
             scalar1=rst[:], scalar2=ngt[:], op0=OP.mult, op1=OP.add),
             reads=[xname, rn, nn], writes=[outname])

    rg = [[0, 1, 2, 3], [4, 5, 6, 7]]
    for half, (lnc, lncname, lnfull, lnfullname) in enumerate(
            [(lncA, "lncA", lnfullA, "lnfullA"), (lncB, "lncB", lnfullB, "lnfullB")]):
        for tt in range(4 * half, 4 * half + 4):
            xb = tt % 2
            p.op("sync", lambda s, tt=tt, xb=xb: s.dma_start(
                out=xt[xb][:], in_=xq[tt * 128:(tt + 1) * 128, :]),
                writes=[f"xt{xb}"], is_dma=True)
            emit_ln(xt[xb], f"xt{xb}", ln1tm, f"ln1tm{tt}", tt * D, par=xb)
        tnames = [f"ln1tm{t_}" for t_ in range(4 * half, 4 * half + 4)]
        for fk4 in range(8):
            pj = fk4 % 2
            pb = pAb if pj == 0 else pBb
            pnames = ["pA0", "pA1"] if pj == 0 else ["pB0", "pB1"]
            for fi in range(4):
                fk = fk4 * 4 + fi
                for ti in range(4):
                    tt = 4 * half + ti
                    p.op("tensor", lambda t, fk=fk, tt=tt, ti=ti, fi=fi, pb=pb: t.transpose(
                        out=pb[:, fi * 512 + ti * 128: fi * 512 + (ti + 1) * 128],
                        in_=ln1tm[:, tt * D + fk * 128: tt * D + (fk + 1) * 128],
                        identity=identb[:]),
                        reads=[tnames[ti], "identb"], writes=pnames[fi // 2:fi // 2 + 1])
            sb = fk4 % 2
            p.op("scalar", lambda s, pb=pb, sb=sb: s.copy(
                out=lnstage[sb][:], in_=pb[:, 0:2048]),
                reads=pnames, writes=[f"lnstage{sb}"])
            p.op("scalar", lambda s, fk4=fk4, sb=sb, lnc=lnc, lncname=lncname: s.dma_start(
                out=lnc.ap()[fk4 * 512:(fk4 + 1) * 512, :].rearrange("(a p) t -> p a t", p=128),
                in_=lnstage[sb][:].rearrange("p (a t) -> p a t", a=4)),
                reads=[f"lnstage{sb}"], writes=[f"{lncname}_{fk4}"], is_dma=True)
        p.op("gpsimd", lambda g, lnc=lnc, lnfull=lnfull: g.collective_compute(
            "AllGather", mybir.AluOpType.bypass, replica_groups=rg,
            ins=[lnc.ap().opt()], outs=[lnfull.ap().opt()]),
            reads=[f"{lncname}_{fk4}" for fk4 in range(8)], writes=[lnfullname], is_cc=True)

    p.fence()
    cur = [D0]
    wqb = alloc("wqb", [128, 8 * 32 * 128], BF16, 65536)
    lnb = [alloc(f"lnb{i}", [128, 32 * 512], BF16, 32768) for i in range(2)]
    wkb = alloc("wkb", [128, 2 * 32 * 128], BF16, 16384)
    wvb = alloc("wvb", [128, 32 * 256], BF16, 16384)
    rc = [alloc("rcx", [128, 512], F32, 2048)] * 2
    rs_ = [alloc("rsx", [128, 512], F32, 2048)] * 2
    rt1 = alloc("rt1", [128, 512], F32, 2048)
    rt2 = alloc("rt2", [128, 512], F32, 2048)
    qst = [alloc(f"qst{i}", [128, 512], BF16, 1024) for i in range(4)]

    p.op("gpsimd", lambda g: g.dma_start(out=wkb[:], in_=wk[:]), writes=["wkb"], is_dma=True)
    p.op("gpsimd", lambda g: g.dma_start(out=wvb[:], in_=wv[:]), writes=["wvb"], is_dma=True)
    p.op("gpsimd", lambda g: g.dma_start(out=wqb[:], in_=wq[:]), writes=["wqb"], is_dma=True)

    sr_list = [(r, 0) for r in range(4)] + [(r, 1) for r in range(4)]

    def emit_rope(psname, ps, coff, outname, outtile, outoff, srt0):
        p.op("vector", lambda v, ps=ps, coff=coff, srt0=srt0: v.tensor_tensor(
            out=rt1[:], in0=ps[:, coff:coff + 512], in1=rc[srt0][:], op=OP.mult),
            reads=[psname, "rcx"], writes=["rt1"])
        p.op("vector", lambda v, ps=ps, coff=coff, srt0=srt0: v.tensor_tensor(
            out=rt2[0:64, :], in0=ps[64:128, coff:coff + 512], in1=rs_[srt0][0:64, :], op=OP.mult),
            reads=[psname, "rsx"], writes=["rt2"])
        p.op("vector", lambda v, ps=ps, coff=coff, srt0=srt0: v.tensor_tensor(
            out=rt2[64:128, :], in0=ps[0:64, coff:coff + 512], in1=rs_[srt0][64:128, :], op=OP.mult),
            reads=[psname, "rsx"], writes=["rt2"])
        p.op("vector", lambda v, outtile=outtile, outoff=outoff: v.tensor_tensor(
            out=outtile[:, outoff:outoff + 512], in0=rt1[:], in1=rt2[:], op=OP.add),
            reads=["rt1", "rt2"], writes=[outname])

    def emit_lnb_load(si):
        rb, th = sr_list[si]
        lb = si % 2
        lnfull = lnfullA if th == 0 else lnfullB
        lname = "lnfullA" if th == 0 else "lnfullB"
        p.op("sync", lambda s, lb=lb, lnfull=lnfull, rb=rb: s.dma_start(
            out=lnb[lb][:].rearrange("p (a t) -> p a t", a=32),
            in_=lnfull.ap()[rb * D:(rb + 1) * D, :].rearrange("(a p) t -> p a t", p=128)),
            reads=[lname], writes=[f"lnb{lb}"], is_dma=True)

    emit_lnb_load(0)
    for si, (rb, th) in enumerate(sr_list):
        t0 = rb * 1024 + th * 512
        lb = si % 2
        srb = si % 2
        p.op("sync", lambda s, srb=srb, t0=t0: s.dma_start(
            out=rc[srb][:], in_=ropec[:, t0:t0 + 512]), writes=["rcx"], is_dma=True)
        p.op("sync", lambda s, srb=srb, t0=t0: s.dma_start(
            out=rs_[srb][:], in_=ropes[:, t0:t0 + 512]), writes=["rsx"], is_dma=True)

        for kc in range(2):
            pj = kc % 2
            for fk in range(32):
                p.op("tensor", lambda t, kc=kc, fk=fk, lb=lb, pj=pj: t.matmul(
                    pC[pj][:], wkb[:, (kc * 32 + fk) * 128:(kc * 32 + fk + 1) * 128],
                    lnb[lb][:, fk * 512:(fk + 1) * 512],
                    start=(fk == 0), stop=(fk == 31)),
                    reads=["wkb", f"lnb{lb}"], writes=[f"pC{pj}"])
            emit_rope(f"pC{pj}", pC[pj], 0, "K", K, kc * T + t0, srb)
        if si + 1 < len(sr_list):
            emit_lnb_load(si + 1)
        for ttl in range(4):
            pj = 2 + ttl % 2
            for fk in range(32):
                p.op("tensor", lambda t, ttl=ttl, fk=fk, lb=lb, pj=pj: t.matmul(
                    pC[pj][:, 0:256],
                    lnb[lb][:, fk * 512 + ttl * 128: fk * 512 + (ttl + 1) * 128],
                    wvb[:, fk * 256:(fk + 1) * 256],
                    start=(fk == 0), stop=(fk == 31)),
                    reads=["wvb", f"lnb{lb}"], writes=[f"pC{pj}"])
            gtt = (t0 // 128) + ttl
            p.op("vector", lambda v, pj=pj, gtt=gtt: v.tensor_copy(
                out=V[:, gtt * 256:(gtt + 1) * 256], in_=pC[pj][:, 0:256]),
                reads=[f"pC{pj}"], writes=["V"])
        for cc in range(8):
            pj = cc % 2
            for fk in range(32):
                p.op("tensor", lambda t, cc=cc, fk=fk, lb=lb, pj=pj: t.matmul(
                    pC[pj][:], wqb[:, (cc * 32 + fk) * 128:(cc * 32 + fk + 1) * 128],
                    lnb[lb][:, fk * 512:(fk + 1) * 512],
                    start=(fk == 0), stop=(fk == 31)),
                    reads=["wqb", f"lnb{lb}"], writes=[f"pC{pj}"])
            qb = cc % 4
            emit_rope(f"pC{pj}", pC[pj], 0, f"qst{qb}", qst[qb], 0, srb)
            p.op("scalar", lambda s, qb=qb: s.copy(out=qst[qb][:], in_=qst[qb][:]),
                 reads=[f"qst{qb}"], writes=[f"qst{qb}"])
            p.op("scalar", lambda s, cc=cc, t0=t0, qb=qb: s.dma_start(
                out=qspill[cc * 128:(cc + 1) * 128, t0:t0 + 512], in_=qst[qb][:]),
                reads=[f"qst{qb}"], writes=[f"qspill_{cc}_{t0}"], is_dma=True)

    p.fence()
    cur = [D0]
    Y = alloc("Y", [128, NQH * T], BF16, 65536)
    qbuf = [alloc(f"qbuf{i}", [128, T], BF16, 8192) for i in range(2)]
    NPB = 4
    Pb = [alloc(f"P{i}", [128, 1024], BF16, 2048) for i in range(NPB)]
    maskb = alloc("maskb", [128, 2048], F32, 8192)
    recb = [alloc(f"rec{i}", [128, 512], F32, 2048) for i in range(2)]

    p.op("sync", lambda s: s.dma_start(out=maskb[:], in_=maskin[:]),
         writes=["maskb"], is_dma=True)
    p.op("scalar", lambda s: s.dma_start(out=qbuf[0][:], in_=qspill[0:128, :]),
         reads=[f"qspill_0_{t0}" for t0 in range(0, T, 512)], writes=["qbuf0"], is_dma=True)

    batches = []
    for qh in range(NQH):
        for qc in range(NSR):
            for sc in range(qc + 1):
                for half in range(2):
                    batches.append((qh, qc, sc, half,
                                    sc == 0 and half == 0,
                                    sc == qc and half == 1))

    def att_qk(i):
        qh, qc, sc, half, first, last = batches[i]
        kv = qh // 4
        pj = i % 2
        pqk = pA if pj == 0 else pB
        pn = ["pA0", "pA1"] if pj == 0 else ["pB0", "pB1"]
        qb = qh % 2
        if qc == 2 and sc == 0 and half == 0 and qh + 1 < NQH:
            p.op("scalar", lambda s, qh=qh: s.dma_start(
                out=qbuf[(qh + 1) % 2][:],
                in_=qspill[(qh + 1) * 128:(qh + 2) * 128, :]),
                reads=[f"qspill_{qh + 1}_{t0}" for t0 in range(0, T, 512)],
                writes=[f"qbuf{(qh + 1) % 2}"], is_dma=True)
        for k2 in range(2):
            sub = sc * 4 + half * 2 + k2
            p.op("tensor", lambda t, sub=sub, k2=k2, pqk=pqk, qc=qc, kv=kv, qb=qb: t.matmul(
                pqk[:, k2 * 512:(k2 + 1) * 512],
                K[:, kv * T + sub * 128: kv * T + (sub + 1) * 128],
                qbuf[qb][:, qc * 512:(qc + 1) * 512],
                start=True, stop=True),
                reads=["K", f"qbuf{qb}"], writes=[pn[k2]])

    def att_exp(i):
        qh, qc, sc, half, first, last = batches[i]
        pj = i % 2
        pqk = pA if pj == 0 else pB
        pn = ["pA0", "pA1"] if pj == 0 else ["pB0", "pB1"]
        pp = i % NPB
        p.op("scalar", lambda s, pqk=pqk, pp=pp: s.activation(
            out=Pb[pp][:], in_=pqk[:], func=AF.Exp, bias=biasc[:], scale=SCALING),
            reads=pn + ["biasc"], writes=[f"P{pp}"])
        if sc == qc:
            p.op("vector", lambda v, pp=pp, half=half: v.tensor_tensor(
                out=Pb[pp][:], in0=Pb[pp][:],
                in1=maskb[:, half * 1024:(half + 1) * 1024], op=OP.mult),
                reads=[f"P{pp}", "maskb"], writes=[f"P{pp}"])

    def att_av(i):
        qh, qc, sc, half, first, last = batches[i]
        kv = qh // 4
        pp = i % NPB
        ypj = qc % 2
        for k2 in range(2):
            sub = sc * 4 + half * 2 + k2
            p.op("tensor", lambda t, sub=sub, pp=pp, ypj=ypj, first=first, last=last, k2=k2, kv=kv: t.matmul(
                pC[ypj][:], V[:, sub * 256 + kv * 128: sub * 256 + (kv + 1) * 128],
                Pb[pp][:, k2 * 512:(k2 + 1) * 512],
                start=(first and k2 == 0), stop=(last and k2 == 1)),
                reads=[f"P{pp}", "V"], writes=[f"pC{ypj}"])
            p.op("tensor", lambda t, pp=pp, ypj=ypj, first=first, last=last, k2=k2: t.matmul(
                pC[2 + ypj][:], ones[:], Pb[pp][:, k2 * 512:(k2 + 1) * 512],
                start=(first and k2 == 0), stop=(last and k2 == 1)),
                reads=[f"P{pp}", "ones"], writes=[f"pC{2 + ypj}"])
        if last:
            p.op("vector", lambda v, ypj=ypj: v.reciprocal(
                out=recb[ypj][:], in_=pC[2 + ypj][:]),
                reads=[f"pC{2 + ypj}"], writes=[f"rec{ypj}"])
            p.op("vector", lambda v, ypj=ypj, qh=qh, qc=qc: v.tensor_tensor(
                out=Y[:, qh * T + qc * 512: qh * T + (qc + 1) * 512],
                in0=pC[ypj][:], in1=recb[ypj][:], op=OP.mult),
                reads=[f"pC{ypj}", f"rec{ypj}"], writes=["Y"])

    att_qk(0)
    att_exp(0)
    att_qk(1)
    for i in range(len(batches)):
        if i + 1 < len(batches):
            att_exp(i + 1)
        if i + 2 < len(batches):
            att_qk(i + 2)
        att_av(i)

    p.fence()
    cur = [D0 + 65536]
    wob = alloc("wob", [128, NCK * 8 * 512], BF16, NCK * 8192)
    ost = [alloc(f"ost{i}", [128, 2048], F32, 8192) for i in range(4)]

    p.op("gpsimd", lambda g: g.dma_start(
        out=wob[:, 0:4096], in_=wo[:, 0, :]), writes=["wob0"], is_dma=True)
    for ck in range(NCK):
        for tg in range(8):
            if tg == 1 and ck + 1 < NCK:
                p.op("gpsimd", lambda g, ck=ck: g.dma_start(
                    out=wob[:, (ck + 1) * 4096:(ck + 2) * 4096], in_=wo[:, ck + 1, :]),
                    writes=[f"wob{ck + 1}"], is_dma=True)
            sj = tg % 4
            for ti in range(4):
                tt = tg * 4 + ti
                slot = (tg * 4 + ti) % 8
                if slot < 4:
                    pt, ptn = pC[slot][:], f"pC{slot}"
                else:
                    base = pA if slot < 6 else pB
                    off = (slot % 2) * 512
                    pt = base[:, off:off + 512]
                    ptn = ["pA0", "pA1", "pB0", "pB1"][slot - 4]
                for fk in range(8):
                    p.op("tensor", lambda t, tt=tt, fk=fk, ck=ck, pt=pt: t.matmul(
                        pt, Y[:, fk * T + tt * 128: fk * T + (tt + 1) * 128],
                        wob[:, (ck * 8 + fk) * 512:(ck * 8 + fk + 1) * 512],
                        start=(fk == 0), stop=(fk == 7)),
                        reads=["Y", f"wob{ck}"], writes=[ptn])
                p.op("scalar", lambda s, ti=ti, sj=sj, pt=pt: s.copy(
                    out=ost[sj][:, ti * 512:(ti + 1) * 512], in_=pt),
                    reads=[ptn], writes=[f"ost{sj}"])
            q = "scalar" if tg % 2 == 0 else "sync"
            p.op(q, lambda s, ck=ck, tg=tg, sj=sj: s.dma_start(
                out=pck[ck][tg], in_=ost[sj][:]),
                reads=[f"ost{sj}"], writes=[f"pck{ck}_{tg}"], is_dma=True)
        p.op("gpsimd", lambda g, ck=ck: g.collective_compute(
            "ReduceScatter", mybir.AluOpType.add, replica_groups=rg,
            ins=[pck[ck].ap().opt()], outs=[rck[ck].ap().opt()]),
            reads=[f"pck{ck}_{tg}" for tg in range(8)], writes=[f"rck{ck}"], is_cc=True)

    p.fence()
    cur = [D0]
    fxt = [alloc(f"fxt{i}", [128, D], F32, 16384) for i in range(2)]
    fat = [alloc(f"fat{i}", [128, D], F32, 16384) for i in range(2)]
    x2t = [alloc(f"x2t{i}", [128, D], F32, 16384) for i in range(2)]
    ln2tm = [alloc(f"ln2tm{i}", [128, D], BF16, 8192) for i in range(2)]
    ln2fm = [alloc(f"ln2fm{i}", [128, 32 * 128], BF16, 8192) for i in range(2)]
    awrb = alloc("awrb", [128, 32 * 132], BF16, 8448)
    bstb = alloc("bstb", [128, 4096], BF16, 8192)
    outst = fxt
    lgt = [alloc(f"lgt{i}", [128, 4], F32, 64) for i in range(2)]
    m1t = [alloc(f"m1t{i}", [128, 1], F32, 64) for i in range(2)]
    nm1t = [alloc(f"nm1t{i}", [128, 1], F32, 64) for i in range(2)]
    l2t = [alloc(f"l2t{i}", [128, 4], F32, 64) for i in range(2)]
    m2t = [alloc(f"m2t{i}", [128, 1], F32, 64) for i in range(2)]
    e2t = [alloc(f"e2t{i}", [128, 1], F32, 64) for i in range(2)]
    rdt = [alloc(f"rdt{i}", [128, 1], F32, 64) for i in range(2)]
    ge1t = [alloc(f"ge1t{i}", [128, 4], F32, 64) for i in range(2)]
    ge2t = [alloc(f"ge2t{i}", [128, 4], F32, 64) for i in range(2)]
    gnt = [alloc(f"gnt{i}", [128, 4], F32, 64) for i in range(2)]
    gtt_ = [alloc(f"gt{i}", [128, 4], F32, 64) for i in range(2)]
    midt = [alloc(f"midt{i}", [128, 128], BF16, 256) for i in range(2)]
    midf = [alloc(f"midf{i}", [128, 128], BF16, 256) for i in range(2)]

    p.op("gpsimd", lambda g: g.dma_start(out=awrb[:], in_=awr[:]), writes=["awrb"], is_dma=True)
    p.op("gpsimd", lambda g: g.dma_start(out=bstb[:], in_=bst[:]), writes=["bstb"], is_dma=True)

    for tt in range(NTT):
        db = tt % 2
        p.op("sync", lambda s, tt=tt, db=db: s.dma_start(
            out=fxt[db][:], in_=xq[tt * 128:(tt + 1) * 128, :]),
            writes=[f"fxt{db}"], is_dma=True)
        for ck in range(NCK):
            q = "scalar" if ck % 2 == 0 else "sync"
            p.op(q, lambda s, tt=tt, ck=ck, db=db: s.dma_start(
                out=fat[db][:, ck * 512:(ck + 1) * 512],
                in_=rck[ck][tt // 4, :, (tt % 4) * 512:(tt % 4 + 1) * 512]),
                reads=[f"rck{ck}"], writes=[f"fat{db}_{ck}"], is_dma=True)
        p.op("vector", lambda v, db=db: v.tensor_tensor(
            out=x2t[db][:], in0=fxt[db][:], in1=fat[db][:], op=OP.add),
            reads=[f"fxt{db}"] + [f"fat{db}_{ck}" for ck in range(NCK)],
            writes=[f"x2t{db}"])
        emit_ln(x2t[db], f"x2t{db}", ln2tm[db], f"ln2tm{db}", 0, par=db)
        for g8 in range(4):
            pj = (g8 + db) % 2
            pb = pAb if pj == 0 else pBb
            pn = ["pA0"] if pj == 0 else ["pB0"]
            for j in range(8):
                p.op("tensor", lambda t, g8=g8, j=j, pb=pb, db=db: t.transpose(
                    out=pb[:, j * 128:(j + 1) * 128],
                    in_=ln2tm[db][:, (g8 * 8 + j) * 128:(g8 * 8 + j + 1) * 128],
                    identity=identb[:]),
                    reads=[f"ln2tm{db}", "identb"], writes=pn)
            p.op("vector", lambda v, g8=g8, pb=pb, db=db: v.tensor_copy(
                out=ln2fm[db][:, g8 * 1024:(g8 + 1) * 1024], in_=pb[:, 0:1024]),
                reads=pn, writes=[f"ln2fm{db}"])
        pm = db
        pmn = f"pC{pm}"
        _lg, _m1, _nm1, _l2, _m2, _e2 = lgt[db], m1t[db], nm1t[db], l2t[db], m2t[db], e2t[db]
        _rd, _g1, _g2, _gn, _gt = rdt[db], ge1t[db], ge2t[db], gnt[db], gtt_[db]
        _mt, _mf = midt[db], midf[db]
        lgn, m1n, nm1n, l2n, m2n, e2n = (f"lgt{db}", f"m1t{db}", f"nm1t{db}",
                                         f"l2t{db}", f"m2t{db}", f"e2t{db}")
        rdn, g1n, g2n, gnn, gtn = (f"rdt{db}", f"ge1t{db}", f"ge2t{db}",
                                   f"gnt{db}", f"gt{db}")
        mtn, mfn = f"midt{db}", f"midf{db}"
        for fk in range(32):
            p.op("tensor", lambda t, fk=fk, pm=pm, db=db: t.matmul(
                pC[pm][:, 0:132], ln2fm[db][:, fk * 128:(fk + 1) * 128],
                awrb[:, fk * 132:(fk + 1) * 132],
                start=(fk == 0), stop=(fk == 31)),
                reads=[f"ln2fm{db}", "awrb"], writes=[pmn])
        p.op("vector", lambda v, pm=pm, _lg=_lg: v.tensor_copy(out=_lg[:], in_=pC[pm][:, 128:132]),
             reads=[pmn], writes=[lgn])
        p.op("vector", lambda v, _lg=_lg, _m1=_m1: v.reduce_max(out=_m1[:], in_=_lg[:], axis=AX.X),
             reads=[lgn], writes=[m1n])
        p.op("vector", lambda v, _m1=_m1, _nm1=_nm1: v.tensor_scalar(out=_nm1[:], in0=_m1[:],
             scalar1=-1.0, scalar2=None, op0=OP.mult), reads=[m1n], writes=[nm1n])
        p.op("vector", lambda v, _lg=_lg, _m1=_m1, _g1=_g1: v.tensor_scalar(out=_g1[:], in0=_lg[:],
             scalar1=_m1[:], scalar2=None, op0=OP.is_ge), reads=[lgn, m1n], writes=[g1n])
        p.op("vector", lambda v, _g1=_g1, _lg=_lg, _l2=_l2: v.scalar_tensor_tensor(out=_l2[:], in0=_g1[:],
             scalar=-1e30, in1=_lg[:], op0=OP.mult, op1=OP.add),
             reads=[g1n, lgn], writes=[l2n])
        p.op("vector", lambda v, _l2=_l2, _m2=_m2: v.reduce_max(out=_m2[:], in_=_l2[:], axis=AX.X),
             reads=[l2n], writes=[m2n])
        p.op("scalar", lambda s, _m2=_m2, _e2=_e2, _nm1=_nm1: s.activation(out=_e2[:], in_=_m2[:], func=AF.Exp,
             bias=_nm1[:]), reads=[m2n, nm1n], writes=[e2n])
        p.op("vector", lambda v, _e2=_e2, _rd=_rd: v.tensor_scalar(out=_rd[:], in0=_e2[:],
             scalar1=1.0, scalar2=None, op0=OP.add), reads=[e2n], writes=[rdn])
        p.op("vector", lambda v, _rd=_rd: v.reciprocal(out=_rd[:], in_=_rd[:]),
             reads=[rdn], writes=[rdn])
        p.op("scalar", lambda s, _lg=_lg, _gn=_gn, _nm1=_nm1: s.activation(out=_gn[:], in_=_lg[:],
             func=AF.Exp, bias=_nm1[:]), reads=[lgn, nm1n], writes=[gnn])
        p.op("vector", lambda v, _lg=_lg, _m2=_m2, _g2=_g2: v.tensor_scalar(out=_g2[:], in0=_lg[:],
             scalar1=_m2[:], scalar2=None, op0=OP.is_ge), reads=[lgn, m2n], writes=[g2n])
        p.op("vector", lambda v, _gn=_gn, _g2=_g2: v.tensor_tensor(out=_gn[:], in0=_gn[:], in1=_g2[:],
             op=OP.mult), reads=[gnn, g2n], writes=[gnn])
        p.op("vector", lambda v, _gn=_gn, _rd=_rd, _gt=_gt: v.tensor_scalar(out=_gt[:], in0=_gn[:],
             scalar1=_rd[:], scalar2=4.0, op0=OP.mult, op1=OP.mult),
             reads=[gnn, rdn], writes=[gtn])
        for e in range(4):
            p.op("vector", lambda v, e=e, pm=pm, _gt=_gt, _mt=_mt: v.tensor_scalar(
                out=_mt[:, e * 32:(e + 1) * 32], in0=pC[pm][:, e * 32:(e + 1) * 32],
                scalar1=_gt[:, e:e + 1], scalar2=None, op0=OP.mult),
                reads=[pmn, gtn], writes=[mtn])
        trp = pAb if db == 0 else pBb
        trn = "pA0" if db == 0 else "pB0"
        p.op("tensor", lambda t, trp=trp, _mt=_mt: t.transpose(out=trp[:, 0:128], in_=_mt[:],
             identity=identb[:]), reads=[mtn, "identb"], writes=[trn])
        p.op("vector", lambda v, trp=trp, _mf=_mf: v.tensor_copy(out=_mf[:], in_=trp[:, 0:128]),
             reads=[trn], writes=[mfn])
        for ck in range(NCK):
            pj = 2 + ck % 2
            p.op("tensor", lambda t, ck=ck, pj=pj, _mf=_mf: t.matmul(
                pC[pj][:], _mf[:], bstb[:, ck * 512:(ck + 1) * 512],
                start=True, stop=True), reads=[mfn, "bstb"], writes=[f"pC{pj}"])
            p.op("vector", lambda v, ck=ck, pj=pj, db=db: v.tensor_tensor(
                out=outst[db][:, ck * 512:(ck + 1) * 512], in0=pC[pj][:],
                in1=x2t[db][:, ck * 512:(ck + 1) * 512], op=OP.add),
                reads=[f"pC{pj}", f"x2t{db}"], writes=[f"fxt{db}"])
        p.op("scalar", lambda s, db=db: s.copy(out=outst[db][:], in_=outst[db][:]),
             reads=[f"fxt{db}"], writes=[f"fxt{db}"])
        p.op("sync", lambda s, tt=tt, db=db: s.dma_start(
            out=out[tt * 128:(tt + 1) * 128, :], in_=outst[db][:]),
            reads=[f"fxt{db}"], writes=["outdram"], is_dma=True)

    stack2 = p.build(end_engine="sync")
    return nc, (st, stack2)


def _host_prep(x, w_qkv, w_o, A, Bm, Wr):
    w_qkv = np.asarray(w_qkv, dtype=np.float32)
    w_o = np.asarray(w_o, dtype=np.float32)
    A = np.asarray(A, dtype=np.float32)
    Bm = np.asarray(Bm, dtype=np.float32)
    Wr = np.asarray(Wr, dtype=np.float32)
    x = np.asarray(x, dtype=np.float32)

    inv = 1.0 / (10000.0 ** (np.arange(0, HD, 2, dtype=np.float32) / HD))
    pos = np.arange(T, dtype=np.float32)
    freqs = np.outer(pos, inv)
    emb = np.concatenate([freqs, freqs], axis=1)
    cosT = np.cos(emb).T.astype(np.float32).copy()
    sinT = np.sin(emb).T.astype(np.float32)
    sinT = sinT.copy()
    sinT[0:64, :] *= -1.0
    mask = np.zeros((128, 2048), dtype=np.float32)
    qidx = np.arange(512)
    for k in range(4):
        for s in range(128):
            mask[s, k * 512:(k + 1) * 512] = (k * 128 + s) <= qidx
    ident = np.eye(128, dtype=np.float32)
    AWr = np.zeros((D, 132), dtype=np.float32)
    for e in range(4):
        AWr[:, e * 32:e * 32 + 16] = A[e].T
        AWr[:, 128 + e] = Wr[e]
    awr_h = np.ascontiguousarray(AWr.reshape(32, 128, 132).transpose(1, 0, 2)
                                 ).reshape(128, 32 * 132)
    bstk = np.zeros((128, D), dtype=np.float32)
    for e in range(4):
        bstk[e * 32:e * 32 + 16, :] = Bm[e].T
    in_maps = []
    for c in range(8):
        b, r = c // 4, c % 4
        wq_h = np.ascontiguousarray(
            w_qkv[:, 1024 * r:1024 * (r + 1)].reshape(32, 128, 8, 128)
            .transpose(1, 2, 0, 3)).reshape(128, 8 * 32 * 128)
        wk_h = np.ascontiguousarray(
            w_qkv[:, 4096 + 256 * r:4096 + 256 * (r + 1)].reshape(32, 128, 2, 128)
            .transpose(1, 2, 0, 3)).reshape(128, 2 * 32 * 128)
        wv_h = np.ascontiguousarray(
            w_qkv[:, 5120 + 256 * r:5120 + 256 * (r + 1)].reshape(32, 128, 256)
            .transpose(1, 0, 2)).reshape(128, 32 * 256)
        wo_h = np.ascontiguousarray(
            w_o[1024 * r:1024 * (r + 1), :].reshape(8, 128, 8, 512)
            .transpose(1, 2, 0, 3)).reshape(128, 8, 8 * 512)
        in_maps.append({
            "xq": np.ascontiguousarray(x[b, r * 1024:(r + 1) * 1024, :]),
            "wq": wq_h, "wk": wk_h, "wv": wv_h, "wo": wo_h,
            "awr": awr_h, "bst": bstk,
            "ropec": cosT, "ropes": sinT, "maskin": mask, "identin": ident,
        })
    return in_maps


def kernel(L0_attn_hidden_states, L0_attn_w_qkv, L0_attn_w_o,
           L0_ffn_A, L0_ffn_B, L0_ffn_Wr):
    in_maps = _host_prep(L0_attn_hidden_states, L0_attn_w_qkv, L0_attn_w_o,
                         L0_ffn_A, L0_ffn_B, L0_ffn_Wr)
    nc, stacks = build_nc()
    res = run_bass_kernel_spmd(nc, in_maps, core_ids=list(range(8)))
    outp = np.empty((B, T, D), dtype=np.float32)
    for c in range(8):
        b, r = c // 4, c % 4
        outp[b, r * 1024:(r + 1) * 1024, :] = res.results[c]["out"]
    return outp


if __name__ == "__main__":
    import reference as _ref
    inputs = _ref.setup_inputs()
    got = kernel(**{k: np.asarray(v) for k, v in inputs.items()})
    exp = np.asarray(_ref.reference(**inputs))
    num = float(np.mean((got - exp) ** 2))
    den = float(np.mean(exp ** 2))
    print("residual variance ratio:", num / den)


# version 31
# speedup vs baseline: 1.3416x; 1.3050x over previous
import contextlib

import numpy as np
import concourse.bass as bass
import concourse.mybir as mybir
from concourse.bass_utils import run_bass_kernel_spmd
from concourse.alu_op_type import AluOpType as OP

F32 = mybir.dt.float32
BF16 = mybir.dt.bfloat16
AF = mybir.ActivationFunctionType
AX = mybir.AxisListType

B, T, D = 2, 4096, 4096
NH, NKV, HD = 32, 8, 128
SCALING = HD ** -0.5
CSH = -14.0
EPS = 1e-5
TQ = 1024
NTT = TQ // 128
NSR = T // 512
NQH = 8
NCK = 8

ENGINES = ("tensor", "vector", "scalar", "gpsimd", "sync")


class Prog:

    RQ = 16

    def __init__(self, nc):
        self.nc = nc
        self.ops = []
        self.eng_count = {e: 0 for e in ENGINES}
        self.dma_count = {e: 0 for e in ENGINES}
        self.qsem_count = {}
        self.cc_count = 0
        self.hwm = {e: {} for e in ENGINES}
        self.bufs = {}
        self._fence = {}

    def buf(self, name):
        if name not in self.bufs:
            self.bufs[name] = [None, []]
        return self.bufs[name]

    def fence(self):
        for e in ENGINES:
            if self.eng_count[e]:
                self._fence["e_" + e] = self.eng_count[e]
        for s, c in self.qsem_count.items():
            self._fence[s] = c
        if self.cc_count:
            self._fence["cc"] = self.cc_count
        self._fence = dict(self._fence)

    def op(self, engine, fn, reads=(), writes=(), is_dma=False, is_cc=False,
           extra_waits=()):
        waits = {}

        def add(tok, waw_self=False):
            if tok is None:
                return
            s, c = tok
            if waw_self and is_dma and s.startswith(f"q_{engine}_"):
                return
            if waits.get(s, 0) < c:
                waits[s] = c

        for b in reads:
            add(self.buf(b)[0])
        for b in writes:
            bb = self.buf(b)
            add(bb[0], waw_self=True)
            for t in bb[1]:
                add(t)
        for t in extra_waits:
            add(t)
        for s, c in self._fence.items():
            add((s, c))
        if is_cc:
            tok = ("cc", self.cc_count + 1)
            self.cc_count += 1
        elif is_dma:
            i = self.dma_count[engine]
            sem = f"q_{engine}_{i % self.RQ}"
            prev = self.qsem_count.get(sem, 0)
            if prev:
                add((sem, prev))
            tok = (sem, prev + 16)
            self.qsem_count[sem] = prev + 16
            self.dma_count[engine] += 1
        else:
            tok = ("e_" + engine, self.eng_count[engine] + 1)
            self.eng_count[engine] += 1
        hw = self.hwm[engine]
        final = []
        for s, c in waits.items():
            if engine == "tensor" and s == "e_tensor":
                continue
            if hw.get(s, 0) >= c:
                continue
            final.append((s, c))
            hw[s] = c
        self.ops.append((engine, fn, final, tok, is_dma or is_cc))
        for b in writes:
            bb = self.buf(b)
            bb[0] = tok
            bb[1] = []
        for b in reads:
            self.buf(b)[1].append(tok)
        return tok

    def build(self, end_engine="sync"):
        nc = self.nc
        stack = contextlib.ExitStack()
        sems = {}
        for e in ENGINES:
            if self.eng_count[e]:
                sems["e_" + e] = stack.enter_context(nc.semaphore("e_" + e))
        for s in self.qsem_count:
            sems[s] = stack.enter_context(nc.semaphore(s))
        if self.cc_count:
            sems["cc"] = stack.enter_context(nc.semaphore("cc"))
        finals = {}
        for e in ENGINES:
            if self.eng_count[e]:
                finals["e_" + e] = self.eng_count[e]
        for s, c in self.qsem_count.items():
            finals[s] = c
        if self.cc_count:
            finals["cc"] = self.cc_count
        per_eng = {e: [] for e in ENGINES}
        for rec in self.ops:
            per_eng[rec[0]].append(rec)
        block = stack.enter_context(nc.Block())

        def make_body(eng_name, recs):
            def body(eng):
                for _, fn, fwaits, tok, is_dma in recs:
                    for s, c in fwaits:
                        eng.wait_ge(sems[s], c)
                    instr = fn(eng)
                    instr.then_inc(sems[tok[0]],
                                   16 if (is_dma and tok[0].startswith("q_")) else 1)
                if eng_name == end_engine:
                    for s, c in finals.items():
                        if s == "e_" + end_engine:
                            continue
                    for s, c in finals.items():
                        if s == "e_" + end_engine:
                            continue
                        eng.wait_ge(sems[s], c)
            return body

        for e in ENGINES:
            if per_eng[e] or e == end_engine:
                getattr(block, e)(make_body(e, per_eng[e]))
        return stack


def build_nc():
    nc = bass.Bass("TRN2", target_bir_lowering=False, debug=False)

    xq = nc.dram_tensor("xq", [TQ, D], F32, kind="ExternalInput").ap()
    wq = nc.dram_tensor("wq", [128, 8 * 32 * 128], F32, kind="ExternalInput").ap()
    wk = nc.dram_tensor("wk", [128, 2 * 32 * 128], F32, kind="ExternalInput").ap()
    wv = nc.dram_tensor("wv", [128, 32 * 256], F32, kind="ExternalInput").ap()
    wo = nc.dram_tensor("wo", [128, 8, 8 * 512], F32, kind="ExternalInput").ap()
    awr = nc.dram_tensor("awr", [128, 32 * 132], F32, kind="ExternalInput").ap()
    bst = nc.dram_tensor("bst", [128, 4096], F32, kind="ExternalInput").ap()
    ropec = nc.dram_tensor("ropec", [128, T], F32, kind="ExternalInput").ap()
    ropes = nc.dram_tensor("ropes", [128, T], F32, kind="ExternalInput").ap()
    maskin = nc.dram_tensor("maskin", [128, 2048], F32, kind="ExternalInput").ap()
    identin = nc.dram_tensor("identin", [128, 128], F32, kind="ExternalInput").ap()
    out = nc.dram_tensor("out", [TQ, D], F32, kind="ExternalOutput").ap()

    lnfullA = nc.dram_tensor("lnfullA", [4 * D, 512], BF16)
    lnfullB = nc.dram_tensor("lnfullB", [4 * D, 512], BF16)
    lncA = nc.dram_tensor("lncA", [D, 512], BF16)
    lncB = nc.dram_tensor("lncB", [D, 512], BF16)
    qspill = nc.dram_tensor("qspill", [NQH * 128, T], BF16)
    pck = [nc.dram_tensor(f"pck{k}", [8, 128, 2048], F32) for k in range(NCK)]
    rck = [nc.dram_tensor(f"rck{k}", [2, 128, 2048], F32) for k in range(NCK)]

    p = Prog(nc)
    st = contextlib.ExitStack()

    P0 = 16512
    def at(name, shape, dtype, off):
        return nc.alloc_sbuf_tensor_at(name, shape, dtype, offset=off)

    K = at("K", [128, 2 * T], BF16, P0)
    V = at("V", [128, 32 * 256], BF16, P0 + 16384)
    SM = P0 + 32768
    ones = at("ones", [128, 128], BF16, SM)
    biasc = at("biasc", [128, 1], F32, SM + 256)
    epst = at("epst", [128, 1], F32, SM + 288)
    stats = [at(f"stats{i}", [128, 48], F32, SM + 320 + 192 * i) for i in range(2)]
    mv = [at(f"mv{i}", [128, 2], F32, SM + 704 + 32 * i) for i in range(2)]
    rstd = [at(f"rstd{i}", [128, 1], F32, SM + 768 + 32 * i) for i in range(2)]
    negm = [at(f"negm{i}", [128, 1], F32, SM + 832 + 32 * i) for i in range(2)]
    identb = at("identb", [128, 128], BF16, SM + 896)
    D0 = SM + 1280
    DEND = 229300

    pA = st.enter_context(nc.psum_tensor("pA", [128, 1024], F32))
    pB = st.enter_context(nc.psum_tensor("pB", [128, 1024], F32))
    pC = [st.enter_context(nc.psum_tensor(f"pC{i}", [128, 512], F32)) for i in range(4)]
    pAb = pA[:].bitcast(BF16)
    pBb = pB[:].bitcast(BF16)

    p.op("vector", lambda v: v.memset(ones[:], 1.0), writes=["ones"])
    p.op("vector", lambda v: v.memset(biasc[:], CSH), writes=["biasc"])
    p.op("vector", lambda v: v.memset(epst[:], EPS), writes=["epst"])
    p.op("gpsimd", lambda g: g.dma_start(out=identb[:], in_=identin[:]),
         writes=["identb"], is_dma=True)

    cur = [D0]
    def alloc(name, shape, dtype, nbytes):
        nbytes = (nbytes + 31) // 32 * 32
        t = at(name, shape, dtype, cur[0])
        cur[0] += nbytes
        assert cur[0] <= DEND, f"SBUF overflow at {name}: {cur[0]}"
        return t

    xt = [alloc(f"xt{i}", [128, D], F32, 16384) for i in range(2)]
    ln1tm = alloc("ln1tm", [128, NTT * D], BF16, NTT * D * 2 // 1)
    lnstage = [alloc(f"lnstage{i}", [128, 2048], BF16, 4096) for i in range(2)]

    def emit_ln(xtile, xname, outtile, outname, outoff, par=0):
        stt, mvt, rst, ngt = stats[par], mv[par], rstd[par], negm[par]
        sn, mn, rn, nn = f"stats{par}", f"mv{par}", f"rstd{par}", f"negm{par}"
        for i in range(8):
            p.op("vector", lambda v, i=i, xtile=xtile, stt=stt: v.bn_stats(
                out=stt[:, i * 6:(i + 1) * 6], in_=xtile[:, i * 512:(i + 1) * 512]),
                reads=[xname], writes=[sn])
        p.op("vector", lambda v, stt=stt, mvt=mvt: v.bn_aggr(out=mvt[:], in_=stt[:]),
             reads=[sn], writes=[mn])
        p.op("scalar", lambda s, mvt=mvt, rst=rst: s.activation(out=rst[:], in_=mvt[:, 1:2],
             func=AF.Sqrt, bias=epst[:]), reads=[mn, "epst"], writes=[rn])
        p.op("vector", lambda v, rst=rst: v.reciprocal(out=rst[:], in_=rst[:]),
             reads=[rn], writes=[rn])
        p.op("vector", lambda v, mvt=mvt, rst=rst, ngt=ngt: v.tensor_scalar(out=ngt[:], in0=mvt[:, 0:1],
             scalar1=rst[:], scalar2=-1.0, op0=OP.mult, op1=OP.mult),
             reads=[mn, rn], writes=[nn])
        p.op("vector", lambda v, xtile=xtile, outtile=outtile, outoff=outoff, rst=rst, ngt=ngt:
             v.tensor_scalar(out=outtile[:, outoff:outoff + D], in0=xtile[:],
             scalar1=rst[:], scalar2=ngt[:], op0=OP.mult, op1=OP.add),
             reads=[xname, rn, nn], writes=[outname])

    rg = [[0, 1, 2, 3], [4, 5, 6, 7]]
    for half, (lnc, lncname, lnfull, lnfullname) in enumerate(
            [(lncA, "lncA", lnfullA, "lnfullA"), (lncB, "lncB", lnfullB, "lnfullB")]):
        for tt in range(4 * half, 4 * half + 4):
            xb = tt % 2
            p.op("sync", lambda s, tt=tt, xb=xb: s.dma_start(
                out=xt[xb][:], in_=xq[tt * 128:(tt + 1) * 128, :]),
                writes=[f"xt{xb}"], is_dma=True)
            emit_ln(xt[xb], f"xt{xb}", ln1tm, f"ln1tm{tt}", tt * D, par=xb)
        tnames = [f"ln1tm{t_}" for t_ in range(4 * half, 4 * half + 4)]
        for fk4 in range(8):
            pj = fk4 % 2
            pb = pAb if pj == 0 else pBb
            pnames = ["pA0", "pA1"] if pj == 0 else ["pB0", "pB1"]
            for fi in range(4):
                fk = fk4 * 4 + fi
                for ti in range(4):
                    tt = 4 * half + ti
                    p.op("tensor", lambda t, fk=fk, tt=tt, ti=ti, fi=fi, pb=pb: t.transpose(
                        out=pb[:, fi * 512 + ti * 128: fi * 512 + (ti + 1) * 128],
                        in_=ln1tm[:, tt * D + fk * 128: tt * D + (fk + 1) * 128],
                        identity=identb[:]),
                        reads=[tnames[ti], "identb"], writes=pnames[fi // 2:fi // 2 + 1])
            sb = fk4 % 2
            p.op("scalar", lambda s, pb=pb, sb=sb: s.copy(
                out=lnstage[sb][:], in_=pb[:, 0:2048]),
                reads=pnames, writes=[f"lnstage{sb}"])
            p.op("scalar", lambda s, fk4=fk4, sb=sb, lnc=lnc, lncname=lncname: s.dma_start(
                out=lnc.ap()[fk4 * 512:(fk4 + 1) * 512, :].rearrange("(a p) t -> p a t", p=128),
                in_=lnstage[sb][:].rearrange("p (a t) -> p a t", a=4)),
                reads=[f"lnstage{sb}"], writes=[f"{lncname}_{fk4}"], is_dma=True)
        p.op("gpsimd", lambda g, lnc=lnc, lnfull=lnfull: g.collective_compute(
            "AllGather", mybir.AluOpType.bypass, replica_groups=rg,
            ins=[lnc.ap().opt()], outs=[lnfull.ap().opt()]),
            reads=[f"{lncname}_{fk4}" for fk4 in range(8)], writes=[lnfullname], is_cc=True)

    p.fence()
    cur = [D0]
    wqb = alloc("wqb", [128, 8 * 32 * 128], BF16, 65536)
    lnb = [alloc(f"lnb{i}", [128, 32 * 512], BF16, 32768) for i in range(2)]
    wkb = alloc("wkb", [128, 2 * 32 * 128], BF16, 16384)
    wvb = alloc("wvb", [128, 32 * 256], BF16, 16384)
    rc = [alloc("rcx", [128, 512], F32, 2048)] * 2
    rs_ = [alloc("rsx", [128, 512], F32, 2048)] * 2
    rt1 = alloc("rt1", [128, 512], F32, 2048)
    rt2 = alloc("rt2", [128, 512], F32, 2048)
    qst = [alloc(f"qst{i}", [128, 512], BF16, 1024) for i in range(4)]

    p.op("gpsimd", lambda g: g.dma_start(out=wkb[:], in_=wk[:]), writes=["wkb"], is_dma=True)
    p.op("gpsimd", lambda g: g.dma_start(out=wvb[:], in_=wv[:]), writes=["wvb"], is_dma=True)
    p.op("gpsimd", lambda g: g.dma_start(out=wqb[:], in_=wq[:]), writes=["wqb"], is_dma=True)

    sr_list = [(r, 0) for r in range(4)] + [(r, 1) for r in range(4)]

    def emit_rope(psname, ps, coff, outname, outtile, outoff, srt0):
        p.op("vector", lambda v, ps=ps, coff=coff, srt0=srt0: v.tensor_tensor(
            out=rt1[:], in0=ps[:, coff:coff + 512], in1=rc[srt0][:], op=OP.mult),
            reads=[psname, "rcx"], writes=["rt1"])
        p.op("vector", lambda v, ps=ps, coff=coff, srt0=srt0: v.tensor_tensor(
            out=rt2[0:64, :], in0=ps[64:128, coff:coff + 512], in1=rs_[srt0][0:64, :], op=OP.mult),
            reads=[psname, "rsx"], writes=["rt2"])
        p.op("vector", lambda v, ps=ps, coff=coff, srt0=srt0: v.tensor_tensor(
            out=rt2[64:128, :], in0=ps[0:64, coff:coff + 512], in1=rs_[srt0][64:128, :], op=OP.mult),
            reads=[psname, "rsx"], writes=["rt2"])
        p.op("vector", lambda v, outtile=outtile, outoff=outoff: v.tensor_tensor(
            out=outtile[:, outoff:outoff + 512], in0=rt1[:], in1=rt2[:], op=OP.add),
            reads=["rt1", "rt2"], writes=[outname])

    def emit_lnb_load(si):
        rb, th = sr_list[si]
        lb = si % 2
        lnfull = lnfullA if th == 0 else lnfullB
        lname = "lnfullA" if th == 0 else "lnfullB"
        p.op("sync", lambda s, lb=lb, lnfull=lnfull, rb=rb: s.dma_start(
            out=lnb[lb][:].rearrange("p (a t) -> p a t", a=32),
            in_=lnfull.ap()[rb * D:(rb + 1) * D, :].rearrange("(a p) t -> p a t", p=128)),
            reads=[lname], writes=[f"lnb{lb}"], is_dma=True)

    emit_lnb_load(0)
    for si, (rb, th) in enumerate(sr_list):
        t0 = rb * 1024 + th * 512
        lb = si % 2
        srb = si % 2
        p.op("sync", lambda s, srb=srb, t0=t0: s.dma_start(
            out=rc[srb][:], in_=ropec[:, t0:t0 + 512]), writes=["rcx"], is_dma=True)
        p.op("sync", lambda s, srb=srb, t0=t0: s.dma_start(
            out=rs_[srb][:], in_=ropes[:, t0:t0 + 512]), writes=["rsx"], is_dma=True)

        for kc in range(2):
            pj = kc % 2
            for fk in range(32):
                p.op("tensor", lambda t, kc=kc, fk=fk, lb=lb, pj=pj: t.matmul(
                    pC[pj][:], wkb[:, (kc * 32 + fk) * 128:(kc * 32 + fk + 1) * 128],
                    lnb[lb][:, fk * 512:(fk + 1) * 512],
                    start=(fk == 0), stop=(fk == 31)),
                    reads=["wkb", f"lnb{lb}"], writes=[f"pC{pj}"])
            emit_rope(f"pC{pj}", pC[pj], 0, "K", K, kc * T + t0, srb)
        if si + 1 < len(sr_list):
            emit_lnb_load(si + 1)
        for ttl in range(4):
            pj = 2 + ttl % 2
            for fk in range(32):
                p.op("tensor", lambda t, ttl=ttl, fk=fk, lb=lb, pj=pj: t.matmul(
                    pC[pj][:, 0:256],
                    lnb[lb][:, fk * 512 + ttl * 128: fk * 512 + (ttl + 1) * 128],
                    wvb[:, fk * 256:(fk + 1) * 256],
                    start=(fk == 0), stop=(fk == 31)),
                    reads=["wvb", f"lnb{lb}"], writes=[f"pC{pj}"])
            gtt = (t0 // 128) + ttl
            p.op("vector", lambda v, pj=pj, gtt=gtt: v.tensor_copy(
                out=V[:, gtt * 256:(gtt + 1) * 256], in_=pC[pj][:, 0:256]),
                reads=[f"pC{pj}"], writes=["V"])
        for cc in range(8):
            pj = cc % 2
            for fk in range(32):
                p.op("tensor", lambda t, cc=cc, fk=fk, lb=lb, pj=pj: t.matmul(
                    pC[pj][:], wqb[:, (cc * 32 + fk) * 128:(cc * 32 + fk + 1) * 128],
                    lnb[lb][:, fk * 512:(fk + 1) * 512],
                    start=(fk == 0), stop=(fk == 31)),
                    reads=["wqb", f"lnb{lb}"], writes=[f"pC{pj}"])
            qb = cc % 4
            emit_rope(f"pC{pj}", pC[pj], 0, f"qst{qb}", qst[qb], 0, srb)
            p.op("scalar", lambda s, qb=qb: s.copy(out=qst[qb][:], in_=qst[qb][:]),
                 reads=[f"qst{qb}"], writes=[f"qst{qb}"])
            p.op("scalar", lambda s, cc=cc, t0=t0, qb=qb: s.dma_start(
                out=qspill[cc * 128:(cc + 1) * 128, t0:t0 + 512], in_=qst[qb][:]),
                reads=[f"qst{qb}"], writes=[f"qspill_{cc}_{t0}"], is_dma=True)

    p.fence()
    cur = [D0]
    Y = alloc("Y", [128, NQH * T], BF16, 65536)
    qbuf = [alloc(f"qbuf{i}", [128, T], BF16, 8192) for i in range(2)]
    NPB = 4
    Pb = [alloc(f"P{i}", [128, 1024], BF16, 2048) for i in range(NPB)]
    maskb = alloc("maskb", [128, 2048], F32, 8192)
    recb = [alloc(f"rec{i}", [128, 512], F32, 2048) for i in range(2)]

    p.op("sync", lambda s: s.dma_start(out=maskb[:], in_=maskin[:]),
         writes=["maskb"], is_dma=True)
    p.op("scalar", lambda s: s.dma_start(out=qbuf[0][:], in_=qspill[0:128, :]),
         reads=[f"qspill_0_{t0}" for t0 in range(0, T, 512)], writes=["qbuf0"], is_dma=True)

    batches = []
    for qh in range(NQH):
        for qc in range(NSR):
            for sc in range(qc + 1):
                for half in range(2):
                    batches.append((qh, qc, sc, half,
                                    sc == 0 and half == 0,
                                    sc == qc and half == 1))

    def att_qk(i):
        qh, qc, sc, half, first, last = batches[i]
        kv = qh // 4
        pj = i % 2
        pqk = pA if pj == 0 else pB
        pn = ["pA0", "pA1"] if pj == 0 else ["pB0", "pB1"]
        qb = qh % 2
        if qc == 2 and sc == 0 and half == 0 and qh + 1 < NQH:
            p.op("scalar", lambda s, qh=qh: s.dma_start(
                out=qbuf[(qh + 1) % 2][:],
                in_=qspill[(qh + 1) * 128:(qh + 2) * 128, :]),
                reads=[f"qspill_{qh + 1}_{t0}" for t0 in range(0, T, 512)],
                writes=[f"qbuf{(qh + 1) % 2}"], is_dma=True)
        for k2 in range(2):
            sub = sc * 4 + half * 2 + k2
            p.op("tensor", lambda t, sub=sub, k2=k2, pqk=pqk, qc=qc, kv=kv, qb=qb: t.matmul(
                pqk[:, k2 * 512:(k2 + 1) * 512],
                K[:, kv * T + sub * 128: kv * T + (sub + 1) * 128],
                qbuf[qb][:, qc * 512:(qc + 1) * 512],
                start=True, stop=True),
                reads=["K", f"qbuf{qb}"], writes=[pn[k2]])

    def att_exp(i):
        qh, qc, sc, half, first, last = batches[i]
        pj = i % 2
        pqk = pA if pj == 0 else pB
        pn = ["pA0", "pA1"] if pj == 0 else ["pB0", "pB1"]
        pp = i % NPB
        p.op("scalar", lambda s, pqk=pqk, pp=pp: s.activation(
            out=Pb[pp][:], in_=pqk[:], func=AF.Exp, bias=biasc[:], scale=SCALING),
            reads=pn + ["biasc"], writes=[f"P{pp}"])
        if sc == qc:
            p.op("vector", lambda v, pp=pp, half=half: v.tensor_tensor(
                out=Pb[pp][:], in0=Pb[pp][:],
                in1=maskb[:, half * 1024:(half + 1) * 1024], op=OP.mult),
                reads=[f"P{pp}", "maskb"], writes=[f"P{pp}"])

    def att_av(i):
        qh, qc, sc, half, first, last = batches[i]
        kv = qh // 4
        pp = i % NPB
        ypj = qc % 2
        for k2 in range(2):
            sub = sc * 4 + half * 2 + k2
            p.op("tensor", lambda t, sub=sub, pp=pp, ypj=ypj, first=first, last=last, k2=k2, kv=kv: t.matmul(
                pC[ypj][:], V[:, sub * 256 + kv * 128: sub * 256 + (kv + 1) * 128],
                Pb[pp][:, k2 * 512:(k2 + 1) * 512],
                start=(first and k2 == 0), stop=(last and k2 == 1)),
                reads=[f"P{pp}", "V"], writes=[f"pC{ypj}"])
            p.op("tensor", lambda t, pp=pp, ypj=ypj, first=first, last=last, k2=k2: t.matmul(
                pC[2 + ypj][:], ones[:], Pb[pp][:, k2 * 512:(k2 + 1) * 512],
                start=(first and k2 == 0), stop=(last and k2 == 1)),
                reads=[f"P{pp}", "ones"], writes=[f"pC{2 + ypj}"])
        if last:
            p.op("vector", lambda v, ypj=ypj: v.reciprocal(
                out=recb[ypj][:], in_=pC[2 + ypj][:]),
                reads=[f"pC{2 + ypj}"], writes=[f"rec{ypj}"])
            p.op("vector", lambda v, ypj=ypj, qh=qh, qc=qc: v.tensor_tensor(
                out=Y[:, qh * T + qc * 512: qh * T + (qc + 1) * 512],
                in0=pC[ypj][:], in1=recb[ypj][:], op=OP.mult),
                reads=[f"pC{ypj}", f"rec{ypj}"], writes=["Y"])

    att_qk(0)
    att_exp(0)
    att_qk(1)
    for i in range(len(batches)):
        if i + 1 < len(batches):
            att_exp(i + 1)
        if i + 2 < len(batches):
            att_qk(i + 2)
        att_av(i)

    p.fence()
    cur = [D0 + 65536]
    wob = alloc("wob", [128, NCK * 8 * 512], BF16, NCK * 8192)
    ost = [alloc(f"ost{i}", [128, 2048], F32, 8192) for i in range(4)]

    p.op("gpsimd", lambda g: g.dma_start(
        out=wob[:, 0:4096], in_=wo[:, 0, :]), writes=["wob0"], is_dma=True)
    for ck in range(NCK):
        for tg in range(8):
            if tg == 1 and ck + 1 < NCK:
                p.op("gpsimd", lambda g, ck=ck: g.dma_start(
                    out=wob[:, (ck + 1) * 4096:(ck + 2) * 4096], in_=wo[:, ck + 1, :]),
                    writes=[f"wob{ck + 1}"], is_dma=True)
            sj = tg % 4
            for ti in range(4):
                tt = tg * 4 + ti
                slot = (tg * 4 + ti) % 8
                if slot < 4:
                    pt, ptn = pC[slot][:], f"pC{slot}"
                else:
                    base = pA if slot < 6 else pB
                    off = (slot % 2) * 512
                    pt = base[:, off:off + 512]
                    ptn = ["pA0", "pA1", "pB0", "pB1"][slot - 4]
                for fk in range(8):
                    p.op("tensor", lambda t, tt=tt, fk=fk, ck=ck, pt=pt: t.matmul(
                        pt, Y[:, fk * T + tt * 128: fk * T + (tt + 1) * 128],
                        wob[:, (ck * 8 + fk) * 512:(ck * 8 + fk + 1) * 512],
                        start=(fk == 0), stop=(fk == 7)),
                        reads=["Y", f"wob{ck}"], writes=[ptn])
                p.op("scalar", lambda s, ti=ti, sj=sj, pt=pt: s.copy(
                    out=ost[sj][:, ti * 512:(ti + 1) * 512], in_=pt),
                    reads=[ptn], writes=[f"ost{sj}"])
            q = "scalar" if tg % 2 == 0 else "sync"
            p.op(q, lambda s, ck=ck, tg=tg, sj=sj: s.dma_start(
                out=pck[ck][tg], in_=ost[sj][:]),
                reads=[f"ost{sj}"], writes=[f"pck{ck}_{tg}"], is_dma=True)
        p.op("gpsimd", lambda g, ck=ck: g.collective_compute(
            "ReduceScatter", mybir.AluOpType.add, replica_groups=rg,
            ins=[pck[ck].ap().opt()], outs=[rck[ck].ap().opt()]),
            reads=[f"pck{ck}_{tg}" for tg in range(8)], writes=[f"rck{ck}"], is_cc=True)

    p.fence()
    cur = [D0]
    fxt = [alloc(f"fxt{i}", [128, D], F32, 16384) for i in range(2)]
    fat = [alloc(f"fat{i}", [128, D], F32, 16384) for i in range(2)]
    x2t = [alloc(f"x2t{i}", [128, D], F32, 16384) for i in range(2)]
    ln2tm = [alloc(f"ln2tm{i}", [128, D], BF16, 8192) for i in range(2)]
    ln2fm = [alloc(f"ln2fm{i}", [128, 32 * 128], BF16, 8192) for i in range(2)]
    awrb = alloc("awrb", [128, 32 * 132], BF16, 8448)
    bstb = alloc("bstb", [128, 4096], BF16, 8192)
    outst = fxt
    lgt = [alloc(f"lgt{i}", [128, 4], F32, 64) for i in range(2)]
    m1t = [alloc(f"m1t{i}", [128, 1], F32, 64) for i in range(2)]
    nm1t = [alloc(f"nm1t{i}", [128, 1], F32, 64) for i in range(2)]
    l2t = [alloc(f"l2t{i}", [128, 4], F32, 64) for i in range(2)]
    m2t = [alloc(f"m2t{i}", [128, 1], F32, 64) for i in range(2)]
    e2t = [alloc(f"e2t{i}", [128, 1], F32, 64) for i in range(2)]
    rdt = [alloc(f"rdt{i}", [128, 1], F32, 64) for i in range(2)]
    ge1t = [alloc(f"ge1t{i}", [128, 4], F32, 64) for i in range(2)]
    ge2t = [alloc(f"ge2t{i}", [128, 4], F32, 64) for i in range(2)]
    gnt = [alloc(f"gnt{i}", [128, 4], F32, 64) for i in range(2)]
    gtt_ = [alloc(f"gt{i}", [128, 4], F32, 64) for i in range(2)]
    midt = [alloc(f"midt{i}", [128, 128], BF16, 256) for i in range(2)]
    midf = [alloc(f"midf{i}", [128, 128], BF16, 256) for i in range(2)]

    p.op("gpsimd", lambda g: g.dma_start(out=awrb[:], in_=awr[:]), writes=["awrb"], is_dma=True)
    p.op("gpsimd", lambda g: g.dma_start(out=bstb[:], in_=bst[:]), writes=["bstb"], is_dma=True)

    for tt in range(NTT):
        db = tt % 2
        p.op("sync", lambda s, tt=tt, db=db: s.dma_start(
            out=fxt[db][:], in_=xq[tt * 128:(tt + 1) * 128, :]),
            writes=[f"fxt{db}"], is_dma=True)
        for ck in range(NCK):
            q = "scalar" if ck % 2 == 0 else "sync"
            p.op(q, lambda s, tt=tt, ck=ck, db=db: s.dma_start(
                out=fat[db][:, ck * 512:(ck + 1) * 512],
                in_=rck[ck][tt // 4, :, (tt % 4) * 512:(tt % 4 + 1) * 512]),
                reads=[f"rck{ck}"], writes=[f"fat{db}_{ck}"], is_dma=True)
        p.op("vector", lambda v, db=db: v.tensor_tensor(
            out=x2t[db][:], in0=fxt[db][:], in1=fat[db][:], op=OP.add),
            reads=[f"fxt{db}"] + [f"fat{db}_{ck}" for ck in range(NCK)],
            writes=[f"x2t{db}"])
        emit_ln(x2t[db], f"x2t{db}", ln2tm[db], f"ln2tm{db}", 0, par=db)
        for g8 in range(4):
            pj = (g8 + db) % 2
            pb = pAb if pj == 0 else pBb
            pn = ["pA0"] if pj == 0 else ["pB0"]
            for j in range(8):
                p.op("tensor", lambda t, g8=g8, j=j, pb=pb, db=db: t.transpose(
                    out=pb[:, j * 128:(j + 1) * 128],
                    in_=ln2tm[db][:, (g8 * 8 + j) * 128:(g8 * 8 + j + 1) * 128],
                    identity=identb[:]),
                    reads=[f"ln2tm{db}", "identb"], writes=pn)
            p.op("vector", lambda v, g8=g8, pb=pb, db=db: v.tensor_copy(
                out=ln2fm[db][:, g8 * 1024:(g8 + 1) * 1024], in_=pb[:, 0:1024]),
                reads=pn, writes=[f"ln2fm{db}"])
        pm = db
        pmn = f"pC{pm}"
        _lg, _m1, _nm1, _l2, _m2, _e2 = lgt[db], m1t[db], nm1t[db], l2t[db], m2t[db], e2t[db]
        _rd, _g1, _g2, _gn, _gt = rdt[db], ge1t[db], ge2t[db], gnt[db], gtt_[db]
        _mt, _mf = midt[db], midf[db]
        lgn, m1n, nm1n, l2n, m2n, e2n = (f"lgt{db}", f"m1t{db}", f"nm1t{db}",
                                         f"l2t{db}", f"m2t{db}", f"e2t{db}")
        rdn, g1n, g2n, gnn, gtn = (f"rdt{db}", f"ge1t{db}", f"ge2t{db}",
                                   f"gnt{db}", f"gt{db}")
        mtn, mfn = f"midt{db}", f"midf{db}"
        for fk in range(32):
            p.op("tensor", lambda t, fk=fk, pm=pm, db=db: t.matmul(
                pC[pm][:, 0:132], ln2fm[db][:, fk * 128:(fk + 1) * 128],
                awrb[:, fk * 132:(fk + 1) * 132],
                start=(fk == 0), stop=(fk == 31)),
                reads=[f"ln2fm{db}", "awrb"], writes=[pmn])
        p.op("vector", lambda v, pm=pm, _lg=_lg: v.tensor_copy(out=_lg[:], in_=pC[pm][:, 128:132]),
             reads=[pmn], writes=[lgn])
        p.op("vector", lambda v, _lg=_lg, _m1=_m1: v.reduce_max(out=_m1[:], in_=_lg[:], axis=AX.X),
             reads=[lgn], writes=[m1n])
        p.op("vector", lambda v, _m1=_m1, _nm1=_nm1: v.tensor_scalar(out=_nm1[:], in0=_m1[:],
             scalar1=-1.0, scalar2=None, op0=OP.mult), reads=[m1n], writes=[nm1n])
        p.op("vector", lambda v, _lg=_lg, _m1=_m1, _g1=_g1: v.tensor_scalar(out=_g1[:], in0=_lg[:],
             scalar1=_m1[:], scalar2=None, op0=OP.is_ge), reads=[lgn, m1n], writes=[g1n])
        p.op("vector", lambda v, _g1=_g1, _lg=_lg, _l2=_l2: v.scalar_tensor_tensor(out=_l2[:], in0=_g1[:],
             scalar=-1e30, in1=_lg[:], op0=OP.mult, op1=OP.add),
             reads=[g1n, lgn], writes=[l2n])
        p.op("vector", lambda v, _l2=_l2, _m2=_m2: v.reduce_max(out=_m2[:], in_=_l2[:], axis=AX.X),
             reads=[l2n], writes=[m2n])
        p.op("scalar", lambda s, _m2=_m2, _e2=_e2, _nm1=_nm1: s.activation(out=_e2[:], in_=_m2[:], func=AF.Exp,
             bias=_nm1[:]), reads=[m2n, nm1n], writes=[e2n])
        p.op("vector", lambda v, _e2=_e2, _rd=_rd: v.tensor_scalar(out=_rd[:], in0=_e2[:],
             scalar1=1.0, scalar2=None, op0=OP.add), reads=[e2n], writes=[rdn])
        p.op("vector", lambda v, _rd=_rd: v.reciprocal(out=_rd[:], in_=_rd[:]),
             reads=[rdn], writes=[rdn])
        p.op("scalar", lambda s, _lg=_lg, _gn=_gn, _nm1=_nm1: s.activation(out=_gn[:], in_=_lg[:],
             func=AF.Exp, bias=_nm1[:]), reads=[lgn, nm1n], writes=[gnn])
        p.op("vector", lambda v, _lg=_lg, _m2=_m2, _g2=_g2: v.tensor_scalar(out=_g2[:], in0=_lg[:],
             scalar1=_m2[:], scalar2=None, op0=OP.is_ge), reads=[lgn, m2n], writes=[g2n])
        p.op("vector", lambda v, _gn=_gn, _g2=_g2: v.tensor_tensor(out=_gn[:], in0=_gn[:], in1=_g2[:],
             op=OP.mult), reads=[gnn, g2n], writes=[gnn])
        p.op("vector", lambda v, _gn=_gn, _rd=_rd, _gt=_gt: v.tensor_scalar(out=_gt[:], in0=_gn[:],
             scalar1=_rd[:], scalar2=4.0, op0=OP.mult, op1=OP.mult),
             reads=[gnn, rdn], writes=[gtn])
        for e in range(4):
            p.op("vector", lambda v, e=e, pm=pm, _gt=_gt, _mt=_mt: v.tensor_scalar(
                out=_mt[:, e * 32:(e + 1) * 32], in0=pC[pm][:, e * 32:(e + 1) * 32],
                scalar1=_gt[:, e:e + 1], scalar2=None, op0=OP.mult),
                reads=[pmn, gtn], writes=[mtn])
        trp = pAb if db == 0 else pBb
        trn = "pA0" if db == 0 else "pB0"
        p.op("tensor", lambda t, trp=trp, _mt=_mt: t.transpose(out=trp[:, 0:128], in_=_mt[:],
             identity=identb[:]), reads=[mtn, "identb"], writes=[trn])
        p.op("vector", lambda v, trp=trp, _mf=_mf: v.tensor_copy(out=_mf[:], in_=trp[:, 0:128]),
             reads=[trn], writes=[mfn])
        for ck in range(NCK):
            pj = 2 + ck % 2
            p.op("tensor", lambda t, ck=ck, pj=pj, _mf=_mf: t.matmul(
                pC[pj][:], _mf[:], bstb[:, ck * 512:(ck + 1) * 512],
                start=True, stop=True), reads=[mfn, "bstb"], writes=[f"pC{pj}"])
            p.op("vector", lambda v, ck=ck, pj=pj, db=db: v.tensor_tensor(
                out=outst[db][:, ck * 512:(ck + 1) * 512], in0=pC[pj][:],
                in1=x2t[db][:, ck * 512:(ck + 1) * 512], op=OP.add),
                reads=[f"pC{pj}", f"x2t{db}"], writes=[f"fxt{db}"])
        p.op("scalar", lambda s, db=db: s.copy(out=outst[db][:], in_=outst[db][:]),
             reads=[f"fxt{db}"], writes=[f"fxt{db}"])
        p.op("sync", lambda s, tt=tt, db=db: s.dma_start(
            out=out[tt * 128:(tt + 1) * 128, :], in_=outst[db][:]),
            reads=[f"fxt{db}"], writes=["outdram"], is_dma=True)

    stack2 = p.build(end_engine="sync")
    return nc, (st, stack2)


def _host_prep(x, w_qkv, w_o, A, Bm, Wr):
    w_qkv = np.asarray(w_qkv, dtype=np.float32)
    w_o = np.asarray(w_o, dtype=np.float32)
    A = np.asarray(A, dtype=np.float32)
    Bm = np.asarray(Bm, dtype=np.float32)
    Wr = np.asarray(Wr, dtype=np.float32)
    x = np.asarray(x, dtype=np.float32)

    inv = 1.0 / (10000.0 ** (np.arange(0, HD, 2, dtype=np.float32) / HD))
    pos = np.arange(T, dtype=np.float32)
    freqs = np.outer(pos, inv)
    emb = np.concatenate([freqs, freqs], axis=1)
    cosT = np.cos(emb).T.astype(np.float32).copy()
    sinT = np.sin(emb).T.astype(np.float32)
    sinT = sinT.copy()
    sinT[0:64, :] *= -1.0
    mask = np.zeros((128, 2048), dtype=np.float32)
    qidx = np.arange(512)
    for k in range(4):
        for s in range(128):
            mask[s, k * 512:(k + 1) * 512] = (k * 128 + s) <= qidx
    ident = np.eye(128, dtype=np.float32)
    AWr = np.zeros((D, 132), dtype=np.float32)
    for e in range(4):
        AWr[:, e * 32:e * 32 + 16] = A[e].T
        AWr[:, 128 + e] = Wr[e]
    awr_h = np.ascontiguousarray(AWr.reshape(32, 128, 132).transpose(1, 0, 2)
                                 ).reshape(128, 32 * 132)
    bstk = np.zeros((128, D), dtype=np.float32)
    for e in range(4):
        bstk[e * 32:e * 32 + 16, :] = Bm[e].T
    in_maps = []
    for c in range(8):
        b, r = c // 4, c % 4
        wq_h = np.ascontiguousarray(
            w_qkv[:, 1024 * r:1024 * (r + 1)].reshape(32, 128, 8, 128)
            .transpose(1, 2, 0, 3)).reshape(128, 8 * 32 * 128)
        wk_h = np.ascontiguousarray(
            w_qkv[:, 4096 + 256 * r:4096 + 256 * (r + 1)].reshape(32, 128, 2, 128)
            .transpose(1, 2, 0, 3)).reshape(128, 2 * 32 * 128)
        wv_h = np.ascontiguousarray(
            w_qkv[:, 5120 + 256 * r:5120 + 256 * (r + 1)].reshape(32, 128, 256)
            .transpose(1, 0, 2)).reshape(128, 32 * 256)
        wo_h = np.ascontiguousarray(
            w_o[1024 * r:1024 * (r + 1), :].reshape(8, 128, 8, 512)
            .transpose(1, 2, 0, 3)).reshape(128, 8, 8 * 512)
        in_maps.append({
            "xq": np.ascontiguousarray(x[b, r * 1024:(r + 1) * 1024, :]),
            "wq": wq_h, "wk": wk_h, "wv": wv_h, "wo": wo_h,
            "awr": awr_h, "bst": bstk,
            "ropec": cosT, "ropes": sinT, "maskin": mask, "identin": ident,
        })
    return in_maps


def kernel(L0_attn_hidden_states, L0_attn_w_qkv, L0_attn_w_o,
           L0_ffn_A, L0_ffn_B, L0_ffn_Wr):
    in_maps = _host_prep(L0_attn_hidden_states, L0_attn_w_qkv, L0_attn_w_o,
                         L0_ffn_A, L0_ffn_B, L0_ffn_Wr)
    nc, stacks = build_nc()
    res = run_bass_kernel_spmd(nc, in_maps, core_ids=list(range(8)))
    outp = np.empty((B, T, D), dtype=np.float32)
    for c in range(8):
        b, r = c // 4, c % 4
        outp[b, r * 1024:(r + 1) * 1024, :] = res.results[c]["out"]
    return outp


if __name__ == "__main__":
    import reference as _ref
    inputs = _ref.setup_inputs()
    got = kernel(**{k: np.asarray(v) for k, v in inputs.items()})
    exp = np.asarray(_ref.reference(**inputs))
    num = float(np.mean((got - exp) ** 2))
    den = float(np.mean(exp ** 2))
    print("residual variance ratio:", num / den)


# version 32
# speedup vs baseline: 1.5518x; 1.1568x over previous
import contextlib

import numpy as np
import concourse.bass as bass
import concourse.mybir as mybir
from concourse.bass_utils import run_bass_kernel_spmd
from concourse.alu_op_type import AluOpType as OP

F32 = mybir.dt.float32
BF16 = mybir.dt.bfloat16
AF = mybir.ActivationFunctionType
AX = mybir.AxisListType

B, T, D = 2, 4096, 4096
NH, NKV, HD = 32, 8, 128
SCALING = HD ** -0.5
CSH = -14.0
EPS = 1e-5
TQ = 1024
NTT = TQ // 128
NSR = T // 512
NQH = 8
NCK = 8

ENGINES = ("tensor", "vector", "scalar", "gpsimd", "sync")


class Prog:

    RQ = 16

    def __init__(self, nc):
        self.nc = nc
        self.ops = []
        self.eng_count = {e: 0 for e in ENGINES}
        self.dma_count = {e: 0 for e in ENGINES}
        self.qsem_count = {}
        self.cc_count = 0
        self.hwm = {e: {} for e in ENGINES}
        self.bufs = {}
        self._fence = {}

    def buf(self, name):
        if name not in self.bufs:
            self.bufs[name] = [None, []]
        return self.bufs[name]

    def fence(self):
        for e in ENGINES:
            if self.eng_count[e]:
                self._fence["e_" + e] = self.eng_count[e]
        for s, c in self.qsem_count.items():
            self._fence[s] = c
        if self.cc_count:
            self._fence["cc"] = self.cc_count
        self._fence = dict(self._fence)

    def op(self, engine, fn, reads=(), writes=(), is_dma=False, is_cc=False,
           extra_waits=()):
        waits = {}

        def add(tok, waw_self=False):
            if tok is None:
                return
            s, c = tok
            if waw_self and is_dma and s.startswith(f"q_{engine}_"):
                return
            if waits.get(s, 0) < c:
                waits[s] = c

        for b in reads:
            add(self.buf(b)[0])
        for b in writes:
            bb = self.buf(b)
            add(bb[0], waw_self=True)
            for t in bb[1]:
                add(t)
        for t in extra_waits:
            add(t)
        for s, c in self._fence.items():
            add((s, c))
        if is_cc:
            tok = ("cc", self.cc_count + 1)
            self.cc_count += 1
        elif is_dma:
            i = self.dma_count[engine]
            sem = f"q_{engine}_{i % self.RQ}"
            prev = self.qsem_count.get(sem, 0)
            if prev:
                add((sem, prev))
            tok = (sem, prev + 16)
            self.qsem_count[sem] = prev + 16
            self.dma_count[engine] += 1
        else:
            tok = ("e_" + engine, self.eng_count[engine] + 1)
            self.eng_count[engine] += 1
        hw = self.hwm[engine]
        final = []
        for s, c in waits.items():
            if engine == "tensor" and s == "e_tensor":
                continue
            if hw.get(s, 0) >= c:
                continue
            final.append((s, c))
            hw[s] = c
        self.ops.append((engine, fn, final, tok, is_dma or is_cc))
        for b in writes:
            bb = self.buf(b)
            bb[0] = tok
            bb[1] = []
        for b in reads:
            self.buf(b)[1].append(tok)
        return tok

    def build(self, end_engine="sync"):
        nc = self.nc
        stack = contextlib.ExitStack()
        sems = {}
        for e in ENGINES:
            if self.eng_count[e]:
                sems["e_" + e] = stack.enter_context(nc.semaphore("e_" + e))
        for s in self.qsem_count:
            sems[s] = stack.enter_context(nc.semaphore(s))
        if self.cc_count:
            sems["cc"] = stack.enter_context(nc.semaphore("cc"))
        finals = {}
        for e in ENGINES:
            if self.eng_count[e]:
                finals["e_" + e] = self.eng_count[e]
        for s, c in self.qsem_count.items():
            finals[s] = c
        if self.cc_count:
            finals["cc"] = self.cc_count
        per_eng = {e: [] for e in ENGINES}
        for rec in self.ops:
            per_eng[rec[0]].append(rec)
        block = stack.enter_context(nc.Block())

        def make_body(eng_name, recs):
            def body(eng):
                for _, fn, fwaits, tok, is_dma in recs:
                    for s, c in fwaits:
                        eng.wait_ge(sems[s], c)
                    instr = fn(eng)
                    instr.then_inc(sems[tok[0]],
                                   16 if (is_dma and tok[0].startswith("q_")) else 1)
                if eng_name == end_engine:
                    for s, c in finals.items():
                        if s == "e_" + end_engine:
                            continue
                    for s, c in finals.items():
                        if s == "e_" + end_engine:
                            continue
                        eng.wait_ge(sems[s], c)
            return body

        for e in ENGINES:
            if per_eng[e] or e == end_engine:
                getattr(block, e)(make_body(e, per_eng[e]))
        return stack


def build_nc():
    nc = bass.Bass("TRN2", target_bir_lowering=False, debug=False)

    xq = nc.dram_tensor("xq", [TQ, D], F32, kind="ExternalInput").ap()
    wq = nc.dram_tensor("wq", [128, 8 * 32 * 128], F32, kind="ExternalInput").ap()
    wk = nc.dram_tensor("wk", [128, 2 * 32 * 128], F32, kind="ExternalInput").ap()
    wv = nc.dram_tensor("wv", [128, 32 * 256], F32, kind="ExternalInput").ap()
    wo = nc.dram_tensor("wo", [128, 8, 8 * 512], F32, kind="ExternalInput").ap()
    awr = nc.dram_tensor("awr", [128, 32 * 132], F32, kind="ExternalInput").ap()
    bst = nc.dram_tensor("bst", [128, 4096], F32, kind="ExternalInput").ap()
    ropec = nc.dram_tensor("ropec", [128, T], F32, kind="ExternalInput").ap()
    ropes = nc.dram_tensor("ropes", [128, T], F32, kind="ExternalInput").ap()
    maskin = nc.dram_tensor("maskin", [128, 2048], F32, kind="ExternalInput").ap()
    identin = nc.dram_tensor("identin", [128, 128], F32, kind="ExternalInput").ap()
    out = nc.dram_tensor("out", [TQ, D], F32, kind="ExternalOutput").ap()

    lnfullA = nc.dram_tensor("lnfullA", [4 * D, 512], BF16)
    lnfullB = nc.dram_tensor("lnfullB", [4 * D, 512], BF16)
    lncA = nc.dram_tensor("lncA", [D, 512], BF16)
    lncB = nc.dram_tensor("lncB", [D, 512], BF16)
    qspill = nc.dram_tensor("qspill", [NQH * 128, T], BF16)
    pck = [nc.dram_tensor(f"pck{k}", [8, 128, 2048], BF16) for k in range(NCK)]
    rck = [nc.dram_tensor(f"rck{k}", [2, 128, 2048], BF16) for k in range(NCK)]

    p = Prog(nc)
    st = contextlib.ExitStack()

    P0 = 16512
    def at(name, shape, dtype, off):
        return nc.alloc_sbuf_tensor_at(name, shape, dtype, offset=off)

    K = at("K", [128, 2 * T], BF16, P0)
    V = at("V", [128, 32 * 256], BF16, P0 + 16384)
    SM = P0 + 32768
    ones = at("ones", [128, 128], BF16, SM)
    biasc = at("biasc", [128, 1], F32, SM + 256)
    epst = at("epst", [128, 1], F32, SM + 288)
    stats = [at(f"stats{i}", [128, 48], F32, SM + 320 + 192 * i) for i in range(2)]
    mv = [at(f"mv{i}", [128, 2], F32, SM + 704 + 32 * i) for i in range(2)]
    rstd = [at(f"rstd{i}", [128, 1], F32, SM + 768 + 32 * i) for i in range(2)]
    negm = [at(f"negm{i}", [128, 1], F32, SM + 832 + 32 * i) for i in range(2)]
    identb = at("identb", [128, 128], BF16, SM + 896)
    D0 = SM + 1280
    DEND = 229300

    pA = st.enter_context(nc.psum_tensor("pA", [128, 1024], F32))
    pB = st.enter_context(nc.psum_tensor("pB", [128, 1024], F32))
    pC = [st.enter_context(nc.psum_tensor(f"pC{i}", [128, 512], F32)) for i in range(4)]
    pAb = pA[:].bitcast(BF16)
    pBb = pB[:].bitcast(BF16)

    p.op("vector", lambda v: v.memset(ones[:], 1.0), writes=["ones"])
    p.op("vector", lambda v: v.memset(biasc[:], CSH), writes=["biasc"])
    p.op("vector", lambda v: v.memset(epst[:], EPS), writes=["epst"])
    p.op("gpsimd", lambda g: g.dma_start(out=identb[:], in_=identin[:]),
         writes=["identb"], is_dma=True)

    cur = [D0]
    def alloc(name, shape, dtype, nbytes):
        nbytes = (nbytes + 31) // 32 * 32
        t = at(name, shape, dtype, cur[0])
        cur[0] += nbytes
        assert cur[0] <= DEND, f"SBUF overflow at {name}: {cur[0]}"
        return t

    xt = [alloc(f"xt{i}", [128, D], F32, 16384) for i in range(2)]
    ln1tm = alloc("ln1tm", [128, NTT * D], BF16, NTT * D * 2 // 1)
    lnstage = [alloc(f"lnstage{i}", [128, 2048], BF16, 4096) for i in range(2)]

    def emit_ln(xtile, xname, outtile, outname, outoff, par=0):
        stt, mvt, rst, ngt = stats[par], mv[par], rstd[par], negm[par]
        sn, mn, rn, nn = f"stats{par}", f"mv{par}", f"rstd{par}", f"negm{par}"
        for i in range(8):
            p.op("vector", lambda v, i=i, xtile=xtile, stt=stt: v.bn_stats(
                out=stt[:, i * 6:(i + 1) * 6], in_=xtile[:, i * 512:(i + 1) * 512]),
                reads=[xname], writes=[sn])
        p.op("vector", lambda v, stt=stt, mvt=mvt: v.bn_aggr(out=mvt[:], in_=stt[:]),
             reads=[sn], writes=[mn])
        p.op("scalar", lambda s, mvt=mvt, rst=rst: s.activation(out=rst[:], in_=mvt[:, 1:2],
             func=AF.Sqrt, bias=epst[:]), reads=[mn, "epst"], writes=[rn])
        p.op("vector", lambda v, rst=rst: v.reciprocal(out=rst[:], in_=rst[:]),
             reads=[rn], writes=[rn])
        p.op("vector", lambda v, mvt=mvt, rst=rst, ngt=ngt: v.tensor_scalar(out=ngt[:], in0=mvt[:, 0:1],
             scalar1=rst[:], scalar2=-1.0, op0=OP.mult, op1=OP.mult),
             reads=[mn, rn], writes=[nn])
        p.op("vector", lambda v, xtile=xtile, outtile=outtile, outoff=outoff, rst=rst, ngt=ngt:
             v.tensor_scalar(out=outtile[:, outoff:outoff + D], in0=xtile[:],
             scalar1=rst[:], scalar2=ngt[:], op0=OP.mult, op1=OP.add),
             reads=[xname, rn, nn], writes=[outname])

    rg = [[0, 1, 2, 3], [4, 5, 6, 7]]
    for half, (lnc, lncname, lnfull, lnfullname) in enumerate(
            [(lncA, "lncA", lnfullA, "lnfullA"), (lncB, "lncB", lnfullB, "lnfullB")]):
        for tt in range(4 * half, 4 * half + 4):
            xb = tt % 2
            p.op("sync", lambda s, tt=tt, xb=xb: s.dma_start(
                out=xt[xb][:], in_=xq[tt * 128:(tt + 1) * 128, :]),
                writes=[f"xt{xb}"], is_dma=True)
            emit_ln(xt[xb], f"xt{xb}", ln1tm, f"ln1tm{tt}", tt * D, par=xb)
        tnames = [f"ln1tm{t_}" for t_ in range(4 * half, 4 * half + 4)]
        for fk4 in range(8):
            pj = fk4 % 2
            pb = pAb if pj == 0 else pBb
            pnames = ["pA0", "pA1"] if pj == 0 else ["pB0", "pB1"]
            for fi in range(4):
                fk = fk4 * 4 + fi
                for ti in range(4):
                    tt = 4 * half + ti
                    p.op("tensor", lambda t, fk=fk, tt=tt, ti=ti, fi=fi, pb=pb: t.transpose(
                        out=pb[:, fi * 512 + ti * 128: fi * 512 + (ti + 1) * 128],
                        in_=ln1tm[:, tt * D + fk * 128: tt * D + (fk + 1) * 128],
                        identity=identb[:]),
                        reads=[tnames[ti], "identb"], writes=pnames[fi // 2:fi // 2 + 1])
            sb = fk4 % 2
            p.op("scalar", lambda s, pb=pb, sb=sb: s.copy(
                out=lnstage[sb][:], in_=pb[:, 0:2048]),
                reads=pnames, writes=[f"lnstage{sb}"])
            p.op("scalar", lambda s, fk4=fk4, sb=sb, lnc=lnc, lncname=lncname: s.dma_start(
                out=lnc.ap()[fk4 * 512:(fk4 + 1) * 512, :].rearrange("(a p) t -> p a t", p=128),
                in_=lnstage[sb][:].rearrange("p (a t) -> p a t", a=4)),
                reads=[f"lnstage{sb}"], writes=[f"{lncname}_{fk4}"], is_dma=True)
        p.op("gpsimd", lambda g, lnc=lnc, lnfull=lnfull: g.collective_compute(
            "AllGather", mybir.AluOpType.bypass, replica_groups=rg,
            ins=[lnc.ap().opt()], outs=[lnfull.ap().opt()]),
            reads=[f"{lncname}_{fk4}" for fk4 in range(8)], writes=[lnfullname], is_cc=True)

    p.fence()
    cur = [D0]
    wqb = alloc("wqb", [128, 8 * 32 * 128], BF16, 65536)
    lnb = [alloc(f"lnb{i}", [128, 32 * 512], BF16, 32768) for i in range(2)]
    wkb = alloc("wkb", [128, 2 * 32 * 128], BF16, 16384)
    wvb = alloc("wvb", [128, 32 * 256], BF16, 16384)
    rc = [alloc("rcx", [128, 512], F32, 2048)] * 2
    rs_ = [alloc("rsx", [128, 512], F32, 2048)] * 2
    rt1 = alloc("rt1", [128, 512], F32, 2048)
    rt2 = alloc("rt2", [128, 512], F32, 2048)
    qst = [alloc(f"qst{i}", [128, 512], BF16, 1024) for i in range(4)]

    p.op("gpsimd", lambda g: g.dma_start(out=wkb[:], in_=wk[:]), writes=["wkb"], is_dma=True)
    p.op("gpsimd", lambda g: g.dma_start(out=wvb[:], in_=wv[:]), writes=["wvb"], is_dma=True)
    p.op("gpsimd", lambda g: g.dma_start(out=wqb[:], in_=wq[:]), writes=["wqb"], is_dma=True)

    sr_list = [(r, 0) for r in range(4)] + [(r, 1) for r in range(4)]

    def emit_rope(psname, ps, coff, outname, outtile, outoff, srt0):
        p.op("vector", lambda v, ps=ps, coff=coff, srt0=srt0: v.tensor_tensor(
            out=rt1[:], in0=ps[:, coff:coff + 512], in1=rc[srt0][:], op=OP.mult),
            reads=[psname, "rcx"], writes=["rt1"])
        p.op("vector", lambda v, ps=ps, coff=coff, srt0=srt0: v.tensor_tensor(
            out=rt2[0:64, :], in0=ps[64:128, coff:coff + 512], in1=rs_[srt0][0:64, :], op=OP.mult),
            reads=[psname, "rsx"], writes=["rt2"])
        p.op("vector", lambda v, ps=ps, coff=coff, srt0=srt0: v.tensor_tensor(
            out=rt2[64:128, :], in0=ps[0:64, coff:coff + 512], in1=rs_[srt0][64:128, :], op=OP.mult),
            reads=[psname, "rsx"], writes=["rt2"])
        p.op("vector", lambda v, outtile=outtile, outoff=outoff: v.tensor_tensor(
            out=outtile[:, outoff:outoff + 512], in0=rt1[:], in1=rt2[:], op=OP.add),
            reads=["rt1", "rt2"], writes=[outname])

    def emit_lnb_load(si):
        rb, th = sr_list[si]
        lb = si % 2
        lnfull = lnfullA if th == 0 else lnfullB
        lname = "lnfullA" if th == 0 else "lnfullB"
        p.op("sync", lambda s, lb=lb, lnfull=lnfull, rb=rb: s.dma_start(
            out=lnb[lb][:].rearrange("p (a t) -> p a t", a=32),
            in_=lnfull.ap()[rb * D:(rb + 1) * D, :].rearrange("(a p) t -> p a t", p=128)),
            reads=[lname], writes=[f"lnb{lb}"], is_dma=True)

    emit_lnb_load(0)
    for si, (rb, th) in enumerate(sr_list):
        t0 = rb * 1024 + th * 512
        lb = si % 2
        srb = si % 2
        p.op("sync", lambda s, srb=srb, t0=t0: s.dma_start(
            out=rc[srb][:], in_=ropec[:, t0:t0 + 512]), writes=["rcx"], is_dma=True)
        p.op("sync", lambda s, srb=srb, t0=t0: s.dma_start(
            out=rs_[srb][:], in_=ropes[:, t0:t0 + 512]), writes=["rsx"], is_dma=True)

        for kc in range(2):
            pj = kc % 2
            for fk in range(32):
                p.op("tensor", lambda t, kc=kc, fk=fk, lb=lb, pj=pj: t.matmul(
                    pC[pj][:], wkb[:, (kc * 32 + fk) * 128:(kc * 32 + fk + 1) * 128],
                    lnb[lb][:, fk * 512:(fk + 1) * 512],
                    start=(fk == 0), stop=(fk == 31)),
                    reads=["wkb", f"lnb{lb}"], writes=[f"pC{pj}"])
            emit_rope(f"pC{pj}", pC[pj], 0, "K", K, kc * T + t0, srb)
        if si + 1 < len(sr_list):
            emit_lnb_load(si + 1)
        for ttl in range(4):
            pj = 2 + ttl % 2
            for fk in range(32):
                p.op("tensor", lambda t, ttl=ttl, fk=fk, lb=lb, pj=pj: t.matmul(
                    pC[pj][:, 0:256],
                    lnb[lb][:, fk * 512 + ttl * 128: fk * 512 + (ttl + 1) * 128],
                    wvb[:, fk * 256:(fk + 1) * 256],
                    start=(fk == 0), stop=(fk == 31)),
                    reads=["wvb", f"lnb{lb}"], writes=[f"pC{pj}"])
            gtt = (t0 // 128) + ttl
            p.op("vector", lambda v, pj=pj, gtt=gtt: v.tensor_copy(
                out=V[:, gtt * 256:(gtt + 1) * 256], in_=pC[pj][:, 0:256]),
                reads=[f"pC{pj}"], writes=["V"])
        for cc in range(8):
            pj = cc % 2
            for fk in range(32):
                p.op("tensor", lambda t, cc=cc, fk=fk, lb=lb, pj=pj: t.matmul(
                    pC[pj][:], wqb[:, (cc * 32 + fk) * 128:(cc * 32 + fk + 1) * 128],
                    lnb[lb][:, fk * 512:(fk + 1) * 512],
                    start=(fk == 0), stop=(fk == 31)),
                    reads=["wqb", f"lnb{lb}"], writes=[f"pC{pj}"])
            qb = cc % 4
            emit_rope(f"pC{pj}", pC[pj], 0, f"qst{qb}", qst[qb], 0, srb)
            p.op("scalar", lambda s, qb=qb: s.copy(out=qst[qb][:], in_=qst[qb][:]),
                 reads=[f"qst{qb}"], writes=[f"qst{qb}"])
            p.op("scalar", lambda s, cc=cc, t0=t0, qb=qb: s.dma_start(
                out=qspill[cc * 128:(cc + 1) * 128, t0:t0 + 512], in_=qst[qb][:]),
                reads=[f"qst{qb}"], writes=[f"qspill_{cc}_{t0}"], is_dma=True)

    p.fence()
    cur = [D0]
    Y = alloc("Y", [128, NQH * T], BF16, 65536)
    qbuf = [alloc(f"qbuf{i}", [128, T], BF16, 8192) for i in range(2)]
    NPB = 4
    Pb = [alloc(f"P{i}", [128, 1024], BF16, 2048) for i in range(NPB)]
    maskb = alloc("maskb", [128, 2048], F32, 8192)
    recb = [alloc(f"rec{i}", [128, 512], F32, 2048) for i in range(2)]

    p.op("sync", lambda s: s.dma_start(out=maskb[:], in_=maskin[:]),
         writes=["maskb"], is_dma=True)
    p.op("scalar", lambda s: s.dma_start(out=qbuf[0][:], in_=qspill[0:128, :]),
         reads=[f"qspill_0_{t0}" for t0 in range(0, T, 512)], writes=["qbuf0"], is_dma=True)

    batches = []
    for qh in range(NQH):
        for qc in range(NSR):
            for sc in range(qc + 1):
                for half in range(2):
                    batches.append((qh, qc, sc, half,
                                    sc == 0 and half == 0,
                                    sc == qc and half == 1))

    def att_qk(i):
        qh, qc, sc, half, first, last = batches[i]
        kv = qh // 4
        pj = i % 2
        pqk = pA if pj == 0 else pB
        pn = ["pA0", "pA1"] if pj == 0 else ["pB0", "pB1"]
        qb = qh % 2
        if qc == 2 and sc == 0 and half == 0 and qh + 1 < NQH:
            p.op("scalar", lambda s, qh=qh: s.dma_start(
                out=qbuf[(qh + 1) % 2][:],
                in_=qspill[(qh + 1) * 128:(qh + 2) * 128, :]),
                reads=[f"qspill_{qh + 1}_{t0}" for t0 in range(0, T, 512)],
                writes=[f"qbuf{(qh + 1) % 2}"], is_dma=True)
        for k2 in range(2):
            sub = sc * 4 + half * 2 + k2
            p.op("tensor", lambda t, sub=sub, k2=k2, pqk=pqk, qc=qc, kv=kv, qb=qb: t.matmul(
                pqk[:, k2 * 512:(k2 + 1) * 512],
                K[:, kv * T + sub * 128: kv * T + (sub + 1) * 128],
                qbuf[qb][:, qc * 512:(qc + 1) * 512],
                start=True, stop=True),
                reads=["K", f"qbuf{qb}"], writes=[pn[k2]])

    def att_exp(i):
        qh, qc, sc, half, first, last = batches[i]
        pj = i % 2
        pqk = pA if pj == 0 else pB
        pn = ["pA0", "pA1"] if pj == 0 else ["pB0", "pB1"]
        pp = i % NPB
        p.op("scalar", lambda s, pqk=pqk, pp=pp: s.activation(
            out=Pb[pp][:], in_=pqk[:], func=AF.Exp, bias=biasc[:], scale=SCALING),
            reads=pn + ["biasc"], writes=[f"P{pp}"])
        if sc == qc:
            p.op("vector", lambda v, pp=pp, half=half: v.tensor_tensor(
                out=Pb[pp][:], in0=Pb[pp][:],
                in1=maskb[:, half * 1024:(half + 1) * 1024], op=OP.mult),
                reads=[f"P{pp}", "maskb"], writes=[f"P{pp}"])

    def att_av(i):
        qh, qc, sc, half, first, last = batches[i]
        kv = qh // 4
        pp = i % NPB
        ypj = qc % 2
        for k2 in range(2):
            sub = sc * 4 + half * 2 + k2
            p.op("tensor", lambda t, sub=sub, pp=pp, ypj=ypj, first=first, last=last, k2=k2, kv=kv: t.matmul(
                pC[ypj][:], V[:, sub * 256 + kv * 128: sub * 256 + (kv + 1) * 128],
                Pb[pp][:, k2 * 512:(k2 + 1) * 512],
                start=(first and k2 == 0), stop=(last and k2 == 1)),
                reads=[f"P{pp}", "V"], writes=[f"pC{ypj}"])
            p.op("tensor", lambda t, pp=pp, ypj=ypj, first=first, last=last, k2=k2: t.matmul(
                pC[2 + ypj][:], ones[:], Pb[pp][:, k2 * 512:(k2 + 1) * 512],
                start=(first and k2 == 0), stop=(last and k2 == 1)),
                reads=[f"P{pp}", "ones"], writes=[f"pC{2 + ypj}"])
        if last:
            p.op("vector", lambda v, ypj=ypj: v.reciprocal(
                out=recb[ypj][:], in_=pC[2 + ypj][:]),
                reads=[f"pC{2 + ypj}"], writes=[f"rec{ypj}"])
            p.op("vector", lambda v, ypj=ypj, qh=qh, qc=qc: v.tensor_tensor(
                out=Y[:, qh * T + qc * 512: qh * T + (qc + 1) * 512],
                in0=pC[ypj][:], in1=recb[ypj][:], op=OP.mult),
                reads=[f"pC{ypj}", f"rec{ypj}"], writes=["Y"])

    att_qk(0)
    att_exp(0)
    att_qk(1)
    for i in range(len(batches)):
        if i + 1 < len(batches):
            att_exp(i + 1)
        if i + 2 < len(batches):
            att_qk(i + 2)
        att_av(i)

    p.fence()
    cur = [D0 + 65536]
    wob = alloc("wob", [128, NCK * 8 * 512], BF16, NCK * 8192)
    ost = [alloc(f"ost{i}", [128, 2048], BF16, 4096) for i in range(4)]

    p.op("gpsimd", lambda g: g.dma_start(
        out=wob[:, 0:4096], in_=wo[:, 0, :]), writes=["wob0"], is_dma=True)
    for ck in range(NCK):
        for tg in range(8):
            if tg == 1 and ck + 1 < NCK:
                p.op("gpsimd", lambda g, ck=ck: g.dma_start(
                    out=wob[:, (ck + 1) * 4096:(ck + 2) * 4096], in_=wo[:, ck + 1, :]),
                    writes=[f"wob{ck + 1}"], is_dma=True)
            sj = tg % 4
            for ti in range(4):
                tt = tg * 4 + ti
                slot = (tg * 4 + ti) % 8
                if slot < 4:
                    pt, ptn = pC[slot][:], f"pC{slot}"
                else:
                    base = pA if slot < 6 else pB
                    off = (slot % 2) * 512
                    pt = base[:, off:off + 512]
                    ptn = ["pA0", "pA1", "pB0", "pB1"][slot - 4]
                for fk in range(8):
                    p.op("tensor", lambda t, tt=tt, fk=fk, ck=ck, pt=pt: t.matmul(
                        pt, Y[:, fk * T + tt * 128: fk * T + (tt + 1) * 128],
                        wob[:, (ck * 8 + fk) * 512:(ck * 8 + fk + 1) * 512],
                        start=(fk == 0), stop=(fk == 7)),
                        reads=["Y", f"wob{ck}"], writes=[ptn])
                p.op("scalar", lambda s, ti=ti, sj=sj, pt=pt: s.copy(
                    out=ost[sj][:, ti * 512:(ti + 1) * 512], in_=pt),
                    reads=[ptn], writes=[f"ost{sj}"])
            q = "scalar" if tg % 2 == 0 else "sync"
            p.op(q, lambda s, ck=ck, tg=tg, sj=sj: s.dma_start(
                out=pck[ck][tg], in_=ost[sj][:]),
                reads=[f"ost{sj}"], writes=[f"pck{ck}_{tg}"], is_dma=True)
        p.op("gpsimd", lambda g, ck=ck: g.collective_compute(
            "ReduceScatter", mybir.AluOpType.add, replica_groups=rg,
            ins=[pck[ck].ap().opt()], outs=[rck[ck].ap().opt()]),
            reads=[f"pck{ck}_{tg}" for tg in range(8)], writes=[f"rck{ck}"], is_cc=True)

    p.fence()
    cur = [D0]
    fxt = [alloc(f"fxt{i}", [128, D], F32, 16384) for i in range(2)]
    fat = [alloc(f"fat{i}", [128, D], F32, 16384) for i in range(2)]
    x2t = [alloc(f"x2t{i}", [128, D], F32, 16384) for i in range(2)]
    ln2tm = [alloc(f"ln2tm{i}", [128, D], BF16, 8192) for i in range(2)]
    ln2fm = [alloc(f"ln2fm{i}", [128, 32 * 128], BF16, 8192) for i in range(2)]
    awrb = alloc("awrb", [128, 32 * 132], BF16, 8448)
    bstb = alloc("bstb", [128, 4096], BF16, 8192)
    outst = fxt
    lgt = [alloc(f"lgt{i}", [128, 4], F32, 64) for i in range(2)]
    m1t = [alloc(f"m1t{i}", [128, 1], F32, 64) for i in range(2)]
    nm1t = [alloc(f"nm1t{i}", [128, 1], F32, 64) for i in range(2)]
    l2t = [alloc(f"l2t{i}", [128, 4], F32, 64) for i in range(2)]
    m2t = [alloc(f"m2t{i}", [128, 1], F32, 64) for i in range(2)]
    e2t = [alloc(f"e2t{i}", [128, 1], F32, 64) for i in range(2)]
    rdt = [alloc(f"rdt{i}", [128, 1], F32, 64) for i in range(2)]
    ge1t = [alloc(f"ge1t{i}", [128, 4], F32, 64) for i in range(2)]
    ge2t = [alloc(f"ge2t{i}", [128, 4], F32, 64) for i in range(2)]
    gnt = [alloc(f"gnt{i}", [128, 4], F32, 64) for i in range(2)]
    gtt_ = [alloc(f"gt{i}", [128, 4], F32, 64) for i in range(2)]
    midt = [alloc(f"midt{i}", [128, 128], BF16, 256) for i in range(2)]
    midf = [alloc(f"midf{i}", [128, 128], BF16, 256) for i in range(2)]

    p.op("gpsimd", lambda g: g.dma_start(out=awrb[:], in_=awr[:]), writes=["awrb"], is_dma=True)
    p.op("gpsimd", lambda g: g.dma_start(out=bstb[:], in_=bst[:]), writes=["bstb"], is_dma=True)

    for tt in range(NTT):
        db = tt % 2
        p.op("sync", lambda s, tt=tt, db=db: s.dma_start(
            out=fxt[db][:], in_=xq[tt * 128:(tt + 1) * 128, :]),
            writes=[f"fxt{db}"], is_dma=True)
        for ck in range(NCK):
            p.op("gpsimd", lambda s, tt=tt, ck=ck, db=db: s.dma_start(
                out=fat[db][:, ck * 512:(ck + 1) * 512],
                in_=rck[ck][tt // 4, :, (tt % 4) * 512:(tt % 4 + 1) * 512]),
                reads=[f"rck{ck}"], writes=[f"fat{db}_{ck}"], is_dma=True)
        p.op("vector", lambda v, db=db: v.tensor_tensor(
            out=x2t[db][:], in0=fxt[db][:], in1=fat[db][:], op=OP.add),
            reads=[f"fxt{db}"] + [f"fat{db}_{ck}" for ck in range(NCK)],
            writes=[f"x2t{db}"])
        emit_ln(x2t[db], f"x2t{db}", ln2tm[db], f"ln2tm{db}", 0, par=db)
        for g8 in range(4):
            pj = (g8 + db) % 2
            pb = pAb if pj == 0 else pBb
            pn = ["pA0"] if pj == 0 else ["pB0"]
            for j in range(8):
                p.op("tensor", lambda t, g8=g8, j=j, pb=pb, db=db: t.transpose(
                    out=pb[:, j * 128:(j + 1) * 128],
                    in_=ln2tm[db][:, (g8 * 8 + j) * 128:(g8 * 8 + j + 1) * 128],
                    identity=identb[:]),
                    reads=[f"ln2tm{db}", "identb"], writes=pn)
            p.op("vector", lambda v, g8=g8, pb=pb, db=db: v.tensor_copy(
                out=ln2fm[db][:, g8 * 1024:(g8 + 1) * 1024], in_=pb[:, 0:1024]),
                reads=pn, writes=[f"ln2fm{db}"])
        pm = db
        pmn = f"pC{pm}"
        _lg, _m1, _nm1, _l2, _m2, _e2 = lgt[db], m1t[db], nm1t[db], l2t[db], m2t[db], e2t[db]
        _rd, _g1, _g2, _gn, _gt = rdt[db], ge1t[db], ge2t[db], gnt[db], gtt_[db]
        _mt, _mf = midt[db], midf[db]
        lgn, m1n, nm1n, l2n, m2n, e2n = (f"lgt{db}", f"m1t{db}", f"nm1t{db}",
                                         f"l2t{db}", f"m2t{db}", f"e2t{db}")
        rdn, g1n, g2n, gnn, gtn = (f"rdt{db}", f"ge1t{db}", f"ge2t{db}",
                                   f"gnt{db}", f"gt{db}")
        mtn, mfn = f"midt{db}", f"midf{db}"
        for fk in range(32):
            p.op("tensor", lambda t, fk=fk, pm=pm, db=db: t.matmul(
                pC[pm][:, 0:132], ln2fm[db][:, fk * 128:(fk + 1) * 128],
                awrb[:, fk * 132:(fk + 1) * 132],
                start=(fk == 0), stop=(fk == 31)),
                reads=[f"ln2fm{db}", "awrb"], writes=[pmn])
        p.op("vector", lambda v, pm=pm, _lg=_lg: v.tensor_copy(out=_lg[:], in_=pC[pm][:, 128:132]),
             reads=[pmn], writes=[lgn])
        p.op("vector", lambda v, _lg=_lg, _m1=_m1: v.reduce_max(out=_m1[:], in_=_lg[:], axis=AX.X),
             reads=[lgn], writes=[m1n])
        p.op("vector", lambda v, _m1=_m1, _nm1=_nm1: v.tensor_scalar(out=_nm1[:], in0=_m1[:],
             scalar1=-1.0, scalar2=None, op0=OP.mult), reads=[m1n], writes=[nm1n])
        p.op("vector", lambda v, _lg=_lg, _m1=_m1, _g1=_g1: v.tensor_scalar(out=_g1[:], in0=_lg[:],
             scalar1=_m1[:], scalar2=None, op0=OP.is_ge), reads=[lgn, m1n], writes=[g1n])
        p.op("vector", lambda v, _g1=_g1, _lg=_lg, _l2=_l2: v.scalar_tensor_tensor(out=_l2[:], in0=_g1[:],
             scalar=-1e30, in1=_lg[:], op0=OP.mult, op1=OP.add),
             reads=[g1n, lgn], writes=[l2n])
        p.op("vector", lambda v, _l2=_l2, _m2=_m2: v.reduce_max(out=_m2[:], in_=_l2[:], axis=AX.X),
             reads=[l2n], writes=[m2n])
        p.op("scalar", lambda s, _m2=_m2, _e2=_e2, _nm1=_nm1: s.activation(out=_e2[:], in_=_m2[:], func=AF.Exp,
             bias=_nm1[:]), reads=[m2n, nm1n], writes=[e2n])
        p.op("vector", lambda v, _e2=_e2, _rd=_rd: v.tensor_scalar(out=_rd[:], in0=_e2[:],
             scalar1=1.0, scalar2=None, op0=OP.add), reads=[e2n], writes=[rdn])
        p.op("vector", lambda v, _rd=_rd: v.reciprocal(out=_rd[:], in_=_rd[:]),
             reads=[rdn], writes=[rdn])
        p.op("scalar", lambda s, _lg=_lg, _gn=_gn, _nm1=_nm1: s.activation(out=_gn[:], in_=_lg[:],
             func=AF.Exp, bias=_nm1[:]), reads=[lgn, nm1n], writes=[gnn])
        p.op("vector", lambda v, _lg=_lg, _m2=_m2, _g2=_g2: v.tensor_scalar(out=_g2[:], in0=_lg[:],
             scalar1=_m2[:], scalar2=None, op0=OP.is_ge), reads=[lgn, m2n], writes=[g2n])
        p.op("vector", lambda v, _gn=_gn, _g2=_g2: v.tensor_tensor(out=_gn[:], in0=_gn[:], in1=_g2[:],
             op=OP.mult), reads=[gnn, g2n], writes=[gnn])
        p.op("vector", lambda v, _gn=_gn, _rd=_rd, _gt=_gt: v.tensor_scalar(out=_gt[:], in0=_gn[:],
             scalar1=_rd[:], scalar2=4.0, op0=OP.mult, op1=OP.mult),
             reads=[gnn, rdn], writes=[gtn])
        for e in range(4):
            p.op("vector", lambda v, e=e, pm=pm, _gt=_gt, _mt=_mt: v.tensor_scalar(
                out=_mt[:, e * 32:(e + 1) * 32], in0=pC[pm][:, e * 32:(e + 1) * 32],
                scalar1=_gt[:, e:e + 1], scalar2=None, op0=OP.mult),
                reads=[pmn, gtn], writes=[mtn])
        trp = pAb if db == 0 else pBb
        trn = "pA0" if db == 0 else "pB0"
        p.op("tensor", lambda t, trp=trp, _mt=_mt: t.transpose(out=trp[:, 0:128], in_=_mt[:],
             identity=identb[:]), reads=[mtn, "identb"], writes=[trn])
        p.op("vector", lambda v, trp=trp, _mf=_mf: v.tensor_copy(out=_mf[:], in_=trp[:, 0:128]),
             reads=[trn], writes=[mfn])
        for ck in range(NCK):
            pj = 2 + ck % 2
            p.op("tensor", lambda t, ck=ck, pj=pj, _mf=_mf: t.matmul(
                pC[pj][:], _mf[:], bstb[:, ck * 512:(ck + 1) * 512],
                start=True, stop=True), reads=[mfn, "bstb"], writes=[f"pC{pj}"])
            p.op("vector", lambda v, ck=ck, pj=pj, db=db: v.tensor_tensor(
                out=outst[db][:, ck * 512:(ck + 1) * 512], in0=pC[pj][:],
                in1=x2t[db][:, ck * 512:(ck + 1) * 512], op=OP.add),
                reads=[f"pC{pj}", f"x2t{db}"], writes=[f"fxt{db}"])
        p.op("scalar", lambda s, db=db: s.copy(out=outst[db][:], in_=outst[db][:]),
             reads=[f"fxt{db}"], writes=[f"fxt{db}"])
        p.op("sync", lambda s, tt=tt, db=db: s.dma_start(
            out=out[tt * 128:(tt + 1) * 128, :], in_=outst[db][:]),
            reads=[f"fxt{db}"], writes=["outdram"], is_dma=True)

    stack2 = p.build(end_engine="sync")
    return nc, (st, stack2)


def _host_prep(x, w_qkv, w_o, A, Bm, Wr):
    w_qkv = np.asarray(w_qkv, dtype=np.float32)
    w_o = np.asarray(w_o, dtype=np.float32)
    A = np.asarray(A, dtype=np.float32)
    Bm = np.asarray(Bm, dtype=np.float32)
    Wr = np.asarray(Wr, dtype=np.float32)
    x = np.asarray(x, dtype=np.float32)

    inv = 1.0 / (10000.0 ** (np.arange(0, HD, 2, dtype=np.float32) / HD))
    pos = np.arange(T, dtype=np.float32)
    freqs = np.outer(pos, inv)
    emb = np.concatenate([freqs, freqs], axis=1)
    cosT = np.cos(emb).T.astype(np.float32).copy()
    sinT = np.sin(emb).T.astype(np.float32)
    sinT = sinT.copy()
    sinT[0:64, :] *= -1.0
    mask = np.zeros((128, 2048), dtype=np.float32)
    qidx = np.arange(512)
    for k in range(4):
        for s in range(128):
            mask[s, k * 512:(k + 1) * 512] = (k * 128 + s) <= qidx
    ident = np.eye(128, dtype=np.float32)
    AWr = np.zeros((D, 132), dtype=np.float32)
    for e in range(4):
        AWr[:, e * 32:e * 32 + 16] = A[e].T
        AWr[:, 128 + e] = Wr[e]
    awr_h = np.ascontiguousarray(AWr.reshape(32, 128, 132).transpose(1, 0, 2)
                                 ).reshape(128, 32 * 132)
    bstk = np.zeros((128, D), dtype=np.float32)
    for e in range(4):
        bstk[e * 32:e * 32 + 16, :] = Bm[e].T
    in_maps = []
    for c in range(8):
        b, r = c // 4, c % 4
        wq_h = np.ascontiguousarray(
            w_qkv[:, 1024 * r:1024 * (r + 1)].reshape(32, 128, 8, 128)
            .transpose(1, 2, 0, 3)).reshape(128, 8 * 32 * 128)
        wk_h = np.ascontiguousarray(
            w_qkv[:, 4096 + 256 * r:4096 + 256 * (r + 1)].reshape(32, 128, 2, 128)
            .transpose(1, 2, 0, 3)).reshape(128, 2 * 32 * 128)
        wv_h = np.ascontiguousarray(
            w_qkv[:, 5120 + 256 * r:5120 + 256 * (r + 1)].reshape(32, 128, 256)
            .transpose(1, 0, 2)).reshape(128, 32 * 256)
        wo_h = np.ascontiguousarray(
            w_o[1024 * r:1024 * (r + 1), :].reshape(8, 128, 8, 512)
            .transpose(1, 2, 0, 3)).reshape(128, 8, 8 * 512)
        in_maps.append({
            "xq": np.ascontiguousarray(x[b, r * 1024:(r + 1) * 1024, :]),
            "wq": wq_h, "wk": wk_h, "wv": wv_h, "wo": wo_h,
            "awr": awr_h, "bst": bstk,
            "ropec": cosT, "ropes": sinT, "maskin": mask, "identin": ident,
        })
    return in_maps


def kernel(L0_attn_hidden_states, L0_attn_w_qkv, L0_attn_w_o,
           L0_ffn_A, L0_ffn_B, L0_ffn_Wr):
    in_maps = _host_prep(L0_attn_hidden_states, L0_attn_w_qkv, L0_attn_w_o,
                         L0_ffn_A, L0_ffn_B, L0_ffn_Wr)
    nc, stacks = build_nc()
    res = run_bass_kernel_spmd(nc, in_maps, core_ids=list(range(8)))
    outp = np.empty((B, T, D), dtype=np.float32)
    for c in range(8):
        b, r = c // 4, c % 4
        outp[b, r * 1024:(r + 1) * 1024, :] = res.results[c]["out"]
    return outp


if __name__ == "__main__":
    import reference as _ref
    inputs = _ref.setup_inputs()
    got = kernel(**{k: np.asarray(v) for k, v in inputs.items()})
    exp = np.asarray(_ref.reference(**inputs))
    num = float(np.mean((got - exp) ** 2))
    den = float(np.mean(exp ** 2))
    print("residual variance ratio:", num / den)
